# version 03
# baseline (speedup 1.0000x reference)
import numpy as np
import ml_dtypes
from contextlib import ExitStack
import concourse.bass as bass
import concourse.mybir as mybir
from concourse.bass_utils import run_bass_kernel_spmd

F32 = mybir.dt.float32
BF16 = mybir.dt.bfloat16
AF = mybir.ActivationFunctionType
ALU = mybir.AluOpType
AX = mybir.AxisListType
BF = ml_dtypes.bfloat16
ENGINES = ("sync", "gpsimd", "scalar", "vector", "tensor")

T = 2048
D = 1024
NCH = D // 128


class Sem:
    __slots__ = ("h", "n")

    def __init__(self, h):
        self.h = h
        self.n = 0

    def bump(self, k=1):
        self.n += k
        return self.n


class Prog:
    def __init__(self, name):
        self.name = name
        self.nc = bass.Bass("TRN2", target_bir_lowering=False, detect_race_conditions=False)
        self.ctx = ExitStack()
        self.qs = {e: [] for e in ENGINES}
        self._i = 0
        self.inputs = {}
        self.outputs = {}
        self._bar = None
        self._barn = 0
        self.PSA = self.ps("PSA", (128, 2048), F32)
        self.PSB = self.ps("PSB", (128, 1024), F32)
        self.PST = self.ps("PST", (128, 1024), BF16)
        self.PSC = self.ps("PSC", (128, 512), F32)

    def inp(self, name, shape, dtype=F32):
        ap = self.nc.dram_tensor(name, list(shape), dtype, kind="ExternalInput").ap()
        self.inputs[name] = shape
        return ap

    def outp(self, name, shape, dtype=F32):
        ap = self.nc.dram_tensor(name, list(shape), dtype, kind="ExternalOutput").ap()
        self.outputs[name] = shape
        return ap

    def sb(self, name, shape, dtype=F32):
        return self.ctx.enter_context(self.nc.sbuf_tensor(name, list(shape), dtype))

    def ps(self, name, shape, dtype=F32):
        return self.ctx.enter_context(self.nc.psum_tensor(name, list(shape), dtype))

    def sem(self, name=None):
        self._i += 1
        return Sem(self.ctx.enter_context(self.nc.semaphore(name or f"s{self._i}")))

    def const(self, arr, name):
        return self.nc.inline_tensor(np.ascontiguousarray(arr), name).ap()

    def q(self, eng, fn, wait=(), inc=None):
        self.qs[eng].append((tuple(wait), fn, inc))

    def dma(self, eng, out_ap, in_ap, sem, wait=()):
        v = sem.bump(16)
        self.q(eng, lambda e, o=out_ap, i=in_ap: e.dma_start(out=o, in_=i),
               wait=wait, inc=(sem, 16))
        return v

    def barrier(self):
        if self._bar is None:
            self._bar = self.sem("bar")
        self._barn += 1
        tgt = self._barn * len(ENGINES)
        for e in ENGINES:
            if e in ("sync",):
                self.q(e, lambda en: en.nop(), inc=(self._bar, 1))
            else:
                self.q(e, lambda en: en.drain(), inc=(self._bar, 1))
            self.q(e, None, wait=((self._bar, tgt),))
        self._bar.n = tgt

    def build(self):
        block = self.ctx.enter_context(self.nc.Block())
        for eng in ENGINES:
            items = self.qs[eng]
            if not items:
                continue

            def runner(e, items=items):
                for waits, fn, inc in items:
                    for (s, v) in waits:
                        e.wait_ge(s.h, v)
                    ins = fn(e) if fn is not None else None
                    if inc is not None:
                        s, k = inc
                        ins.then_inc(s.h, k)
            getattr(block, eng)(runner)
        self.ctx.close()
        return self.nc


def chunks_of(total, step):
    return [(i, min(step, total - i)) for i in range(0, total, step)]


def mm_tiled(p, *, nK, m_list, col_list, lhsT, rhs, evac, pss,
             mm_wait=None):
    done = [p.sem(), p.sem()]
    free = [p.sem(), p.sem()]
    i = 0
    for (m, mr) in m_list:
        for (c0, w) in col_list:
            b = i % 2
            ps = pss[b]
            waits0 = []
            if i >= 2:
                waits0.append((free[b], free[b].n))
            if mm_wait is not None:
                waits0 += list(mm_wait(m, c0))
            for k in range(nK):
                last = (k == nK - 1)
                inc = None
                if last:
                    done[b].bump(1)
                    inc = (done[b], 1)
                p.q("tensor",
                    lambda e, k=k, m=m, mr=mr, c0=c0, w=w, ps=ps, last=last:
                        e.matmul(ps[0:mr, 0:w], lhsT(k, m), rhs(k, c0, w),
                                 start=(k == 0), stop=last),
                    wait=tuple(waits0) if k == 0 else (), inc=inc)
            free[b].bump(1)
            evac(ps, m, mr, c0, w, ((done[b], done[b].n),), (free[b], 1))
            i += 1


def ln_scratch(p, bufA, bufB, tag, Tn=T):
    mu = bufA[0:1, 0:2 * Tn].bitcast(F32)
    e2 = bufA[0:1, 2 * Tn:4 * Tn].bitcast(F32)
    r_ = bufA[0:1, 4 * Tn:6 * Tn].bitcast(F32)
    mr = bufA[0:1, 6 * Tn:8 * Tn].bitcast(F32)
    rb = bufB[0:1, 0:Tn]
    mrb = bufB[0:1, Tn:2 * Tn]
    rbc = bufB[:, 2 * Tn:3 * Tn]
    mrbc = bufB[:, 3 * Tn:4 * Tn]
    xf = [bufB[:, 4 * Tn:6 * Tn].bitcast(F32), bufB[:, 6 * Tn:8 * Tn].bitcast(F32)]
    sqb = [p.sb(f"lnsq{tag}{i}", (128, 512), BF16) for i in range(2)]
    return dict(mu=mu, e2=e2, r_=r_, mr=mr, rb=rb, mrb=mrb, rbc=rbc, mrbc=mrbc,
                xf=xf, sqb=sqb)


def emit_ln(p, x_hbm, xn, ones_col, ones_row, scr, Tn=T, eps=1e-5):
    ld = [p.sem(), p.sem()]
    cps = p.sem()
    xf = scr["xf"]
    sqb = scr["sqb"]
    mu = scr["mu"]; e2 = scr["e2"]; r_ = scr["r_"]; mr = scr["mr"]
    rb = scr["rb"]; mrb = scr["mrb"]; rbc = scr["rbc"]; mrbc = scr["mrbc"]
    colgroups = chunks_of(Tn, 512)

    for c in range(NCH):
        b = c % 2
        lv = p.dma("sync", xf[b], x_hbm[c * 128:(c + 1) * 128, :], ld[b],
                   wait=((cps, c - 1),) if c >= 2 else ())
        p.q("vector", lambda e, b=b, c=c: e.tensor_copy(xn[:, c * Tn:(c + 1) * Tn], xf[b]),
            wait=((ld[b], lv),), inc=(cps, 1))
        cps.bump(1)
    p.barrier()
    sqs = p.sem()
    tds = p.sem()
    evs = p.sem()
    psx = p.PSA[0:1, 0:512]
    psq = p.PSA[0:1, 512:1024]
    for gi, (c0, w) in enumerate(colgroups):
        for c in range(NCH):
            b = c % 2
            sv = sqs.bump(1)
            p.q("scalar",
                lambda e, b=b, c=c, c0=c0, w=w: e.activation(
                    out=sqb[b][:, 0:w], in_=xn[:, c * Tn + c0:c * Tn + c0 + w], func=AF.Square),
                wait=((tds, tds.n - 1),) if sv > 2 else (), inc=(sqs, 1))
            p.q("tensor",
                lambda e, c=c, c0=c0, w=w: e.matmul(
                    psx[0:1, 0:w], ones_col[:, 0:1], xn[:, c * Tn + c0:c * Tn + c0 + w],
                    start=(c == 0), stop=(c == NCH - 1)),
                wait=((evs, evs.n),) if (c == 0 and gi > 0) else ())
            tds.bump(1)
            p.q("tensor",
                lambda e, b=b, c=c, w=w: e.matmul(psq[0:1, 0:w], ones_col[:, 0:1], sqb[b][:, 0:w],
                                                  start=(c == 0), stop=(c == NCH - 1)),
                wait=((sqs, sv),) + (((evs, evs.n),) if (c == 0 and gi > 0) else ()),
                inc=(tds, 1))
        dv = tds.n
        ev = evs.bump(1)
        p.q("vector", lambda e, c0=c0, w=w: e.tensor_copy(mu[:, c0:c0 + w], psx[0:1, 0:w]),
            wait=((tds, dv),))
        p.q("vector", lambda e, c0=c0, w=w: e.tensor_copy(e2[:, c0:c0 + w], psq[0:1, 0:w]),
            inc=(evs, 1))
    p.barrier()
    p.q("vector", lambda e: e.tensor_scalar(mu, mu, 1.0 / D, None, ALU.mult, ALU.bypass))
    p.q("vector", lambda e: e.tensor_scalar(e2, e2, 1.0 / D, None, ALU.mult, ALU.bypass))
    p.q("vector", lambda e: e.tensor_mul(r_, mu, mu))
    p.q("vector", lambda e: e.tensor_sub(e2, e2, r_))
    p.q("vector", lambda e: e.tensor_scalar(e2, e2, float(eps), None, ALU.add, ALU.bypass))
    p.barrier()
    p.q("scalar", lambda e: e.activation(out=r_, in_=e2, func=AF.Ln))
    p.barrier()
    p.q("scalar", lambda e: e.activation(out=r_, in_=r_, func=AF.Exp, scale=-0.5))
    p.barrier()
    p.q("vector", lambda e: e.tensor_mul(mr, mu, r_))
    p.q("vector", lambda e: e.tensor_copy(rb, r_))
    p.q("vector", lambda e: e.tensor_copy(mrb, mr))
    p.barrier()
    bmm = p.sem()
    bcp = p.sem()
    for (src, dst) in ((rb, rbc), (mrb, mrbc)):
        for (c0, w) in colgroups:
            mv = bmm.bump(1)
            p.q("tensor", lambda e, src=src, c0=c0, w=w: e.matmul(
                p.PSC[:, 0:w], ones_row[0:1, :], src[:, c0:c0 + w], start=True, stop=True),
                wait=((bcp, bcp.n),) if bcp.n > 0 else (), inc=(bmm, 1))
            bcp.bump(1)
            p.q("vector", lambda e, dst=dst, c0=c0, w=w: e.tensor_copy(
                dst[:, c0:c0 + w], p.PSC[:, 0:w]), wait=((bmm, mv),), inc=(bcp, 1))
    p.barrier()
    for c in range(NCH):
        p.q("vector", lambda e, c=c: e.tensor_mul(xn[:, c * Tn:(c + 1) * Tn],
                                                  xn[:, c * Tn:(c + 1) * Tn], rbc))
    for c in range(NCH):
        p.q("vector", lambda e, c=c: e.tensor_sub(xn[:, c * Tn:(c + 1) * Tn],
                                                  xn[:, c * Tn:(c + 1) * Tn], mrbc))
    p.barrier()
    return dict(mu=mu, e2=e2, r_=r_, mr=mr, rbc=rbc, mrbc=mrbc)


def store_fm(p, sb_tile, hbm, sem, Tn=T, nch=NCH, wait=()):
    for c in range(nch):
        p.dma("sync", hbm[c * 128:(c + 1) * 128, :], sb_tile[:, c * Tn:(c + 1) * Tn],
              sem, wait=wait)


def finish(p, sems):
    for s in sems:
        p.q("sync", None, wait=((s, s.n),))


def build_test_ln():
    p = Prog("test_ln")
    x = p.inp("x", (D, T), F32)
    xn_out = p.outp("xn", (D, T), BF16)
    ones_col = p.sb("ones_col", (128, 1), BF16)
    ones_row = p.sb("ones_row", (1, 128), BF16)
    oc = p.const(np.ones((128, 1), BF), "oc")
    orr = p.const(np.ones((1, 128), BF), "or")
    lds = p.sem()
    p.dma("gpsimd", ones_col[:], oc, lds)
    p.dma("gpsimd", ones_row[:], orr, lds)
    p.q("tensor", None, wait=((lds, 32),))
    p.q("vector", None, wait=((lds, 32),))
    p.q("scalar", None, wait=((lds, 32),))
    xn = p.sb("xn_sb", (128, NCH * T), BF16)
    bufA = p.sb("bufA", (128, NCH * T), BF16)
    bufB = p.sb("bufB", (128, NCH * T), BF16)
    scr = ln_scratch(p, bufA, bufB, "t")
    dd = emit_ln(p, x, xn, ones_col, ones_row, scr)
    st = p.sem()
    store_fm(p, xn, xn_out, st)
    for nm in ("mu", "e2", "r_", "mr"):
        o = p.outp("dbg_" + nm, (1, T), F32)
        p.dma("sync", o, dd[nm], st)
    finish(p, [st])
    return p


def build_l0a(debug=False):
    p = Prog("l0a")
    x = p.inp("x", (D, T), F32)
    WcT = p.inp("WcT", (D, 10240), F32)
    bcT = p.inp("bcT", (128, 16), F32)
    bcv = p.inp("bcv", (1, 8192), F32)
    wrT = p.inp("wrT", (D, 16), F32)
    kfT = p.inp("kfT", (8, 16, 128, 256), F32)
    vf = p.inp("vf", (8, 16, 256, 128), F32)
    wmhT = p.inp("wmhT", (D, D), F32)
    wmgT = p.inp("wmgT", (D, D), F32)

    o_xn = p.outp("xn", (D, T), BF16)
    o_qk = p.outp("qk", (2048, T), BF16)
    o_vt = p.outp("v_tok", (T, 8192), BF16)
    o_g = p.outp("g", (D, T), BF16)

    ones = p.sb("ones", (128, 128), BF16)
    ident = p.sb("ident", (128, 128), BF16)
    ld0 = p.sem()
    p.dma("gpsimd", ones[:], p.const(np.ones((128, 128), BF), "cones"), ld0)
    p.dma("gpsimd", ident[:], p.const(np.eye(128, dtype=BF), "cident"), ld0)

    A16 = p.sb("A16", (128, NCH * T), BF16)
    B16 = p.sb("B16", (128, NCH * T), BF16)
    C16 = p.sb("C16", (128, NCH * T), BF16)
    wmh_sb = p.sb("wmh_sb", (128, NCH * D), BF16)
    wmg_sb = p.sb("wmg_sb", (128, NCH * D), BF16)
    wr_sb = p.sb("wr_sb", (128, NCH * 16), BF16)
    bct_sb = p.sb("bct_sb", (128, 16), F32)
    bcv_sb = None
    logits = p.sb("logits", (128, 16 * 16), F32)
    wcomb = p.sb("wcomb", (128, 16 * 16), F32)
    tmp16 = p.sb("tmp16", (128, 16), F32)
    m1c = p.sb("m1c", (128, 1), F32)
    m2c = p.sb("m2c", (128, 1), F32)

    p.dma("gpsimd", wmh_sb[:].rearrange("p (k n) -> p k n", k=NCH),
          wmhT.rearrange("(k p) n -> p k n", p=128), ld0)
    p.dma("gpsimd", wmg_sb[:].rearrange("p (k n) -> p k n", k=NCH),
          wmgT.rearrange("(k p) n -> p k n", p=128), ld0)
    p.dma("gpsimd", wr_sb[:].rearrange("p (k n) -> p k n", k=NCH),
          wrT.rearrange("(k p) n -> p k n", p=128), ld0)
    p.dma("gpsimd", bct_sb[:], bcT, ld0)
    for e in ENGINES:
        p.q(e, None, wait=((ld0, ld0.n),))

    scr = ln_scratch(p, C16, B16, "a")
    emit_ln(p, x, A16, ones[:, 0:1], ones[0:1, :], scr)
    xn = A16
    st_xn = p.sem()
    store_fm(p, xn, o_xn, st_xn)

    wcq = [p.sb(f"wcq{i}", (128, NCH * 128), BF16) for i in range(2)]
    wcl = [p.sem(), p.sem()]
    wfree = p.sem()
    stq = [p.sem(), p.sem()]
    qstg = [p.sb(f"qstg{i}", (128, T), BF16) for i in range(2)]
    qsc = [p.sb(f"qsc{i}", (128, 512), F32) for i in range(2)]
    evq = p.sem()
    for m in range(16):
        p.dma("gpsimd", wcq[m % 2][:].rearrange("p (k n) -> p k n", k=NCH),
              WcT[:, m * 128:(m + 1) * 128].rearrange("(k p) n -> p k n", p=128),
              wcl[m % 2], wait=((wfree, m - 1),) if m >= 2 else ())

    def evac_q(ps, m, mr, c0, w, wait, freeinc):
        if c0 == 0 and m >= 2:
            wait = wait + ((stq[m % 2], 16 * (m // 2)),)
        b2 = (c0 // 512) % 2
        p.q("vector",
            lambda e, ps=ps, m=m, c0=c0, w=w, b2=b2: e.tensor_scalar(
                qsc[b2][:, 0:w], ps[0:128, 0:w], bct_sb[:, m:m + 1], None,
                ALU.add, ALU.bypass),
            wait=wait, inc=freeinc)
        p.q("vector",
            lambda e, m=m, c0=c0, w=w, b2=b2: e.scalar_tensor_tensor(
                qstg[m % 2][:, c0:c0 + w], qsc[b2][:, 0:w], 0.01, qsc[b2][:, 0:w],
                ALU.mult, ALU.max))
        if c0 == T - 512:
            p.q("tensor", lambda e: e.nop(), inc=(wfree, 1))
            wfree.bump(1)
            ev = evq.bump(1)
            p.q("vector", lambda e: e.drain(), inc=(evq, 1))
            p.dma("sync", o_qk[m * 128:(m + 1) * 128, :], qstg[m % 2][:], stq[m % 2],
                  wait=((evq, ev),))

    mm_tiled(p, nK=NCH, m_list=[(m, 128) for m in range(16)],
             col_list=chunks_of(T, 512),
             lhsT=lambda k, m: wcq[m % 2][:, k * 128:(k + 1) * 128],
             rhs=lambda k, c0, w: xn[:, k * T + c0:k * T + c0 + w],
             evac=evac_q, pss=[p.PSA[:, 0:512], p.PSA[:, 512:1024]],
             mm_wait=lambda m, c0: [(wcl[m % 2], 16 * (m // 2 + 1))] if c0 == 0 else [])
    p.barrier()

    bcv_sb = C16[0:1, 0:8192]
    ldv = p.sem()
    p.dma("gpsimd", bcv_sb, bcv, ldv)
    p.q("tensor", None, wait=((ldv, 16),))
    wcv = [p.sb(f"wcv{i}", (128, NCH * 512), BF16) for i in range(2)]
    wvl = [p.sem(), p.sem()]
    wvfree = p.sem()
    stv = [p.sem(), p.sem()]
    vstg = [p.sb(f"vstg{i}", (128, 512), BF16) for i in range(2)]
    vsc = [p.sb(f"vsc{i}", (128, 512), F32) for i in range(2)]
    evv = p.sem()
    vdone = [p.sem(), p.sem()]
    vfree = [p.sem(), p.sem()]
    vps = [p.PSA[:, 0:512], p.PSA[:, 512:1024]]
    ti = 0
    for vg in range(16):
        p.dma("gpsimd", wcv[vg % 2][:].rearrange("p (k n) -> p k n", k=NCH),
              WcT[:, 2048 + vg * 512:2048 + (vg + 1) * 512].rearrange("(k p) n -> p k n", p=128),
              wvl[vg % 2], wait=((wvfree, vg - 1),) if vg >= 2 else ())
        for t in range(16):
            b = ti % 2
            ps = vps[b]
            w0 = []
            if ti >= 2:
                w0.append((vfree[b], vfree[b].n))
            if t == 0:
                w0.append((wvl[vg % 2], 16 * (vg // 2 + 1)))
            for k in range(NCH):
                p.q("tensor",
                    lambda e, k=k, t=t, vg=vg, ps=ps: e.matmul(
                        ps[0:128, 0:512], xn[:, k * T + t * 128:k * T + (t + 1) * 128],
                        wcv[vg % 2][:, k * 512:(k + 1) * 512],
                        start=(k == 0), stop=False),
                    wait=tuple(w0) if k == 0 else ())
            vdone[b].bump(1)
            p.q("tensor",
                lambda e, vg=vg, ps=ps: e.matmul(
                    ps[0:128, 0:512], ones[0:1, :], bcv_sb[:, vg * 512:(vg + 1) * 512],
                    start=False, stop=True),
                inc=(vdone[b], 1))
            sv = evv.bump(1)
            p.q("vector",
                lambda e, b=b, ps=ps: e.tensor_copy(vsc[b][:], ps[0:128, 0:512]),
                wait=((vdone[b], vdone[b].n),) + (((stv[b], 16 * (ti // 2)),) if ti >= 2 else ()),
                inc=(vfree[b], 1))
            p.q("vector",
                lambda e, b=b: e.scalar_tensor_tensor(
                    vstg[b][:], vsc[b][:], 0.01, vsc[b][:], ALU.mult, ALU.max))
            vfree[b].bump(1)
            p.q("vector", lambda e: e.drain(), inc=(evv, 1))
            p.dma("sync", o_vt[t * 128:(t + 1) * 128, vg * 512:(vg + 1) * 512],
                  vstg[b][:], stv[b], wait=((evv, sv),))
            ti += 1
        p.q("tensor", lambda e: e.nop(), inc=(wvfree, 1))
        wvfree.bump(1)
    p.barrier()

    rds = p.sem()
    rvs = p.sem()
    for t in range(16):
        for k in range(NCH):
            p.q("tensor",
                lambda e, k=k, t=t: e.matmul(p.PSC[:, 0:16],
                                             xn[:, k * T + t * 128:k * T + (t + 1) * 128],
                                             wr_sb[:, k * 16:(k + 1) * 16],
                                             start=(k == 0), stop=(k == NCH - 1)),
                wait=((rvs, rvs.n),) if (k == 0 and t > 0) else (),
                inc=(rds, 1) if k == NCH - 1 else None)
        rds.bump(1)
        rvs.bump(1)
        p.q("vector", lambda e, t=t: e.tensor_copy(logits[:, t * 16:(t + 1) * 16], p.PSC[:, 0:16]),
            wait=((rds, rds.n),), inc=(rvs, 1))
    p.barrier()
    hz = p.sem()
    tmp16b = p.sb("tmp16b", (128, 16), F32)
    for t in range(16):
        lt = logits[:, t * 16:(t + 1) * 16]
        wt = wcomb[:, t * 16:(t + 1) * 16]

        def step(eng, fn):
            v = hz.bump(1)
            p.q(eng, fn)
            p.q(eng, lambda e: e.drain(), inc=(hz, 1))
            for o in ("vector", "scalar"):
                if o != eng:
                    p.q(o, None, wait=((hz, v),))

        step("vector", lambda e, lt=lt: e.reduce_max(m1c[:], lt, axis=AX.X))
        step("vector", lambda e, lt=lt, wt=wt: e.tensor_single_scalar(wt, lt, m1c[:], ALU.is_ge))
        step("vector", lambda e, lt=lt, wt=wt: e.scalar_tensor_tensor(
            tmp16[:], wt, -1e30, lt, ALU.mult, ALU.add))
        step("vector", lambda e: e.reduce_max(m2c[:], tmp16[:], axis=AX.X))
        step("vector", lambda e: e.tensor_single_scalar(tmp16b[:], tmp16[:], m2c[:], ALU.is_ge))
        step("vector", lambda e, wt=wt: e.tensor_add(wt, wt, tmp16b[:]))
        step("scalar", lambda e, lt=lt: e.activation(out=tmp16[:], in_=lt, func=AF.Sigmoid))
        step("vector", lambda e, wt=wt: e.tensor_mul(wt, wt, tmp16[:]))
    p.barrier()

    mm_tiled(p, nK=NCH, m_list=[(m, 128) for m in range(NCH)],
             col_list=chunks_of(T, 512),
             lhsT=lambda k, m: wmh_sb[:, k * D + m * 128:k * D + (m + 1) * 128],
             rhs=lambda k, c0, w: xn[:, k * T + c0:k * T + c0 + w],
             evac=lambda ps, m, mr, c0, w, wait, freeinc: p.q(
                 "vector",
                 lambda e, ps=ps, m=m, c0=c0, w=w: e.tensor_copy(
                     B16[:, m * T + c0:m * T + c0 + w], ps[0:128, 0:w]),
                 wait=wait, inc=freeinc),
             pss=[p.PSA[:, 1024:1536], p.PSA[:, 1536:2048]])
    q_mh = B16
    p.barrier()

    actT = A16
    yc = C16
    p.q("vector", lambda e: e.memset(yc[:], 0.0))
    p.barrier()
    kfb = [p.sb(f"kfb{i}", (128, 8 * 256), BF16) for i in range(2)]
    vfb = [p.sb(f"vfb{i}", (128, 8 * 2 * 128), BF16) for i in range(2)]
    efl = [p.sem(), p.sem()]
    eff = p.sem()
    gels = p.sem()
    sdone = p.sem()
    ycomb = p.sem()
    ydone = [p.sem(), p.sem()]
    yfree = [p.sem(), p.sem()]
    yps = [p.PSB[:, 0:1024], p.PSA[:, 0:1024]]
    sps = [p.PSA[:, 1024:1536], p.PSA[:, 1536:2048]]
    yi = 0
    for e_ in range(16):
        p.dma("gpsimd", kfb[e_ % 2][:].rearrange("p (h s) -> p h s", h=8),
              kfT[:, e_].rearrange("h p s -> p h s"), efl[e_ % 2],
              wait=((eff, e_ - 1),) if e_ >= 2 else ())
        for sc_ in range(2):
            p.dma("gpsimd", vfb[e_ % 2][:, sc_ * 1024:(sc_ + 1) * 1024].rearrange("p (h d) -> p h d", d=128),
                  vf[:, e_, sc_ * 128:(sc_ + 1) * 128, :].rearrange("h p d -> p h d"), efl[e_ % 2])
        for th in range(2):
            n = e_ * 2 + th
            first_gelu = True
            for h in range(8):
                for sc in range(2):
                    for ci in range(2):
                        pbank = sps[(sc * 2 + ci) % 2]
                        gv = gels.bump(1)
                        w0 = []
                        if gv > 2:
                            w0.append((gels, gv - 2))
                        if h == 0 and sc == 0 and ci == 0 and th == 0:
                            w0.append((efl[e_ % 2], 48 * (e_ // 2 + 1)))
                        if first_gelu and n >= 1:
                            w0.append((ycomb, 8 * n))
                        first_gelu = False
                        sdone.bump(1)
                        p.q("tensor",
                            lambda e, e_=e_, h=h, sc=sc, ci=ci, th=th, pbank=pbank:
                                e.matmul(pbank[0:128, 0:512],
                                         kfb[e_ % 2][:, h * 256 + sc * 128:h * 256 + (sc + 1) * 128],
                                         q_mh[:, h * T + th * 1024 + ci * 512:h * T + th * 1024 + (ci + 1) * 512],
                                         start=True, stop=True),
                            wait=tuple(w0), inc=(sdone, 1))
                        p.q("scalar",
                            lambda e, h=h, sc=sc, ci=ci, pbank=pbank: e.activation(
                                out=actT[:, h * T + sc * 1024 + ci * 512:h * T + sc * 1024 + (ci + 1) * 512],
                                in_=pbank[0:128, 0:512], func=AF.Gelu),
                            wait=((sdone, sdone.n),), inc=(gels, 1))
            for tl in range(8):
                tg = th * 8 + tl
                b = yi % 2
                ps = yps[b]
                w0 = [(gels, gels.n)]
                if yi >= 2:
                    w0.append((yfree[b], yfree[b].n))
                if th == 0 and tl == 0:
                    w0.append((efl[e_ % 2], 48 * (e_ // 2 + 1)))
                for h in range(8):
                    for sc in range(2):
                        last = (h == 7 and sc == 1)
                        if last:
                            ydone[b].bump(1)
                        p.q("tensor",
                            lambda e, e_=e_, h=h, sc=sc, tl=tl, ps=ps, last=last:
                                e.matmul(ps[0:128, h * 128:(h + 1) * 128],
                                         actT[:, h * T + sc * 1024 + tl * 128:h * T + sc * 1024 + (tl + 1) * 128],
                                         vfb[e_ % 2][:, sc * 1024 + h * 128:sc * 1024 + (h + 1) * 128],
                                         start=(sc == 0), stop=(sc == 1)),
                            wait=tuple(w0) if (h == 0 and sc == 0) else (),
                            inc=(ydone[b], 1) if last else None)
                yfree[b].bump(1)
                ycomb.bump(1)
                p.q("vector",
                    lambda e, tg=tg, e_=e_, ps=ps: e.scalar_tensor_tensor(
                        yc[:, tg * 1024:(tg + 1) * 1024], ps[0:128, 0:1024],
                        wcomb[:, tg * 16 + e_:tg * 16 + e_ + 1],
                        yc[:, tg * 1024:(tg + 1) * 1024], ALU.mult, ALU.add),
                    wait=((ydone[b], ydone[b].n),), inc=(yfree[b], 1))
                p.q("vector", lambda e: e.engine_nop(), inc=(ycomb, 1))
                yi += 1
        eff.bump(1)
        p.q("tensor", lambda e: e.nop(), inc=(eff, 1))
    p.barrier()

    if debug:
        dbs = p.sem()
        o_wc = p.outp("dbg_wcomb", (128, 256), F32)
        o_yc = p.outp("dbg_yc", (128, NCH * T), BF16)
        o_qmh = p.outp("dbg_qmh", (128, NCH * T), BF16)
        p.dma("sync", o_wc, wcomb[:], dbs)
        p.dma("sync", o_yc, yc[:], dbs)
        p.dma("sync", o_qmh, q_mh[:], dbs)
        p.q("sync", None, wait=((dbs, dbs.n),))
        p.barrier()

    yfm = B16
    trs = p.sem()
    cpd = p.sem()
    for t in range(16):
        for dc in range(NCH):
            last = (dc == NCH - 1)
            if last:
                trs.bump(1)
            p.q("tensor",
                lambda e, t=t, dc=dc: e.transpose(
                    p.PST[:, dc * 128:(dc + 1) * 128],
                    yc[:, t * 1024 + dc * 128:t * 1024 + (dc + 1) * 128], ident[:]),
                wait=((cpd, t),) if (dc == 0 and t >= 1) else (),
                inc=(trs, 1) if last else None)
        cpd.bump(1)
        p.q("vector",
            lambda e, t=t: e.tensor_copy(
                yfm[:].rearrange("p (dc tt) -> p dc tt", dc=NCH)[:, :, t * 128:(t + 1) * 128],
                p.PST[:].rearrange("p (dc tt) -> p dc tt", dc=NCH)),
            wait=((trs, trs.n),), inc=(cpd, 1))
    p.barrier()

    stg = [p.sem(), p.sem()]
    gstg = [p.sb(f"gstg{i}", (128, 512), BF16) for i in range(2)]
    evg = p.sem()
    gti = [0]

    def evac_g(ps, m, mr, c0, w, wait, freeinc):
        i = gti[0]
        b = i % 2
        if i >= 2:
            wait = wait + ((stg[b], 16 * (i // 2)),)
        p.q("scalar",
            lambda e, ps=ps, b=b, w=w: e.activation(out=gstg[b][:, 0:w], in_=ps[0:128, 0:w],
                                                    func=AF.Silu),
            wait=wait, inc=freeinc)
        ev = evg.bump(1)
        p.q("scalar", lambda e: e.drain(), inc=(evg, 1))
        p.dma("sync", o_g[m * 128:(m + 1) * 128, c0:c0 + w], gstg[b][:, 0:w], stg[b],
              wait=((evg, ev),))
        gti[0] += 1

    mm_tiled(p, nK=NCH, m_list=[(m, 128) for m in range(NCH)],
             col_list=chunks_of(T, 512),
             lhsT=lambda k, m: wmg_sb[:, k * D + m * 128:k * D + (m + 1) * 128],
             rhs=lambda k, c0, w: yfm[:, k * T + c0:k * T + c0 + w],
             evac=evac_g, pss=[p.PSB[:, 0:512], p.PSB[:, 512:1024]])
    finish(p, [st_xn, stq[0], stq[1], stv[0], stv[1], stg[0], stg[1]])
    return p


def build_l0b():
    p = Prog("l0b")
    q_h = p.inp("q", (1024, T), BF16)
    kfull = p.inp("kfull", (1024, 4096), BF16)
    vfull = p.inp("vfull", (4096, 8192), BF16)
    WoT = p.inp("WoT", (8192, D), F32)
    boT = p.inp("boT", (128, 8), F32)
    xh = p.inp("x", (D, T), F32)
    xnh = p.inp("xn", (D, T), BF16)
    gh = p.inp("g", (D, T), BF16)
    o_x1 = p.outp("x1", (D, T), F32)

    ones = p.sb("ones", (128, 128), BF16)
    ld0 = p.sem()
    p.dma("gpsimd", ones[:], p.const(np.ones((128, 128), BF), "cones"), ld0)
    kf = p.sb("kf", (128, 8 * 4096), BF16)
    qs = p.sb("qs", (128, 8 * T), BF16)
    xn_sb = p.sb("xn_sb", (128, NCH * T), BF16)
    bo_sb = p.sb("bo_sb", (128, 8), F32)
    p.dma("gpsimd", kf[:].rearrange("p (k n) -> p k n", k=8),
          kfull.rearrange("(k p) n -> p k n", p=128), ld0)
    p.dma("gpsimd", qs[:].rearrange("p (k n) -> p k n", k=8),
          q_h.rearrange("(k p) n -> p k n", p=128), ld0)
    p.dma("gpsimd", xn_sb[:].rearrange("p (k n) -> p k n", k=NCH),
          xnh.rearrange("(k p) n -> p k n", p=128), ld0)
    p.dma("gpsimd", bo_sb[:], boT, ld0)
    for e in ENGINES:
        p.q(e, None, wait=((ld0, ld0.n),))

    vt = [p.sb(f"vt{i}", (128, 512), BF16) for i in range(2)]
    pT = [p.sb(f"pT{i}", (128, 512), BF16) for i in range(2)]
    wob = [p.sb(f"wob{i}", (128, 32 * 128), BF16) for i in range(2)]
    outT = p.sb("outT", (128, 4 * 512), BF16)
    rbc_sb = p.sb("rbc_sb", (128, 512), BF16)
    den_r = p.sb("den_r", (1, 512), F32)
    den_rb = p.sb("den_rb", (1, 512), BF16)
    projacc = p.sb("projacc", (128, 8 * 512), F32)
    xsl = p.sb("xsl", (128, 512), F32)
    gsl = p.sb("gsl", (128, 512), BF16)
    sg1 = p.sb("sg1", (128, 512), F32)
    x1stg = [p.sb(f"x1stg{i}", (128, 512), F32) for i in range(2)]

    psO = [p.PSA[:, i * 512:(i + 1) * 512] for i in range(4)]
    psS = [p.PSB[:, 0:512], p.PSB[:, 512:1024]]
    psDen = p.PSC[0:1, 0:512]
    psP = p.PSC[:, 0:512]

    vl = [p.sem(), p.sem()]
    vu = p.sem()
    eS = p.sem()
    eE = p.sem()
    ePV = p.sem()
    eH = p.sem()
    wl = [p.sem(), p.sem()]
    wu = p.sem()
    ePP = p.sem()
    eAC = p.sem()
    stx = [p.sem(), p.sem()]
    exs = p.sem()
    xls = p.sem()

    SCALE_Q = 1.0 / 8.0
    hi = 0
    for qc in range(4):
        p.q("vector", lambda e: e.memset(projacc[:], 0.0))
        p.barrier()
        for h in range(16):
            hb = (h % 2) * 64
            hc = h // 2
            for sc_ in range(4):
                p.dma("gpsimd", wob[h % 2][:, sc_ * 1024:(sc_ + 1) * 1024].rearrange("p (m n) -> p m n", n=128),
                      WoT[h * 512 + sc_ * 128:h * 512 + (sc_ + 1) * 128, :].rearrange("p (m n) -> p m n", n=128),
                      wl[h % 2], wait=((wu, max(0, hi - 1)),) if (hi >= 2 and sc_ == 0) else ())
            for kc in range(32):
                b = kc % 2
                p.dma("sync", vt[b][:], vfull[kc * 128:(kc + 1) * 128, h * 512:(h + 1) * 512],
                      vl[b], wait=((vu, max(0, (hi * 32 + kc) - 1)),) if hi * 32 + kc >= 2 else ())
                sv = eS.bump(1)
                p.q("tensor",
                    lambda e, hb=hb, hc=hc, kc=kc, qc=qc, b=b: e.matmul(
                        psS[b][0:128, 0:512],
                        kf[hb:hb + 64, hc * 4096 + kc * 128:hc * 4096 + (kc + 1) * 128],
                        qs[hb:hb + 64, hc * T + qc * 512:hc * T + (qc + 1) * 512],
                        start=True, stop=True),
                    wait=((eE, eE.n - 1),) if eE.n >= 1 else (), inc=(eS, 1))
                ev = eE.bump(1)
                p.q("scalar",
                    lambda e, b=b: e.activation(out=pT[b][:], in_=psS[b][0:128, 0:512],
                                                func=AF.Exp, scale=0.125),
                    wait=((eS, sv),) + (((ePV, 5 * (hi * 32 + kc - 1)),) if hi * 32 + kc >= 2 else ()),
                    inc=(eE, 1))
                w0 = ((eE, ev), (vl[b], vl[b].n))
                if kc == 0:
                    w0 = w0 + ((eAC, eAC.n),)
                p.q("tensor",
                    lambda e, b=b, kc=kc: e.matmul(psDen, ones[:, 0:1], pT[b][:],
                                                   start=(kc == 0), stop=(kc == 31)),
                    wait=w0, inc=(ePV, 1))
                ePV.bump(1)
                for dc in range(4):
                    ePV.bump(1)
                    p.q("tensor",
                        lambda e, b=b, dc=dc, kc=kc: e.matmul(
                            psO[dc][0:128, 0:512], vt[b][:, dc * 128:(dc + 1) * 128],
                            pT[b][:], start=(kc == 0), stop=(kc == 31)),
                        inc=(ePV, 1))
                vu.bump(1)
                p.q("tensor", lambda e: e.nop(), inc=(vu, 1))
            hv = eH.bump(1)
            p.q("vector", lambda e: e.tensor_copy(den_r[:], psDen), wait=((ePV, ePV.n),))
            p.q("vector", lambda e: e.drain(), inc=(eH, 1))
            hv2 = eH.bump(1)
            p.q("scalar", lambda e: e.activation(out=den_r[:], in_=den_r[:], func=AF.Ln),
                wait=((eH, hv),))
            p.q("scalar", lambda e: e.activation(out=den_rb[:], in_=den_r[:], func=AF.Exp,
                                                 scale=-1.0))
            p.q("scalar", lambda e: e.drain(), inc=(eH, 1))
            hv3 = eH.bump(1)
            p.q("tensor", lambda e: e.matmul(psS[0][0:128, 0:512], ones[0:1, :], den_rb[:],
                                             start=True, stop=True),
                wait=((eH, hv2),), inc=(eH, 1))
            hv4 = eH.bump(1)
            p.q("vector", lambda e: e.tensor_copy(rbc_sb[:], psS[0][0:128, 0:512]),
                wait=((eH, hv3),))
            for dc in range(4):
                p.q("vector", lambda e, dc=dc: e.tensor_mul(outT[:, dc * 512:(dc + 1) * 512],
                                                            psO[dc][0:128, 0:512], rbc_sb[:]))
            p.q("vector", lambda e: e.drain(), inc=(eH, 1))
            for m in range(8):
                for sc in range(4):
                    last = (sc == 3)
                    if last:
                        ePP.bump(1)
                    p.q("tensor",
                        lambda e, h=h, m=m, sc=sc, last=last: e.matmul(
                            psP[0:128, 0:512],
                            wob[h % 2][:, (sc * 8 + m) * 128:(sc * 8 + m + 1) * 128],
                            outT[:, sc * 512:(sc + 1) * 512],
                            start=(sc == 0), stop=last),
                        wait=((eH, hv4), (wl[h % 2], wl[h % 2].n),
                              (eAC, eAC.n)) if sc == 0 else (),
                        inc=(ePP, 1) if last else None)
                eAC.bump(1)
                p.q("vector",
                    lambda e, m=m: e.tensor_add(projacc[:, m * 512:(m + 1) * 512],
                                                projacc[:, m * 512:(m + 1) * 512],
                                                psP[0:128, 0:512]),
                    wait=((ePP, ePP.n),), inc=(eAC, 1))
            wu.bump(1)
            p.q("tensor", lambda e: e.nop(), inc=(wu, 1))
            hi += 1
        p.barrier()
        for m in range(8):
            bst = (qc * 8 + m)
            b = bst % 2
            xv = p.dma("sync", xsl[:], xh[m * 128:(m + 1) * 128, qc * 512:(qc + 1) * 512],
                       xls, wait=((exs, exs.n),) if bst >= 1 else ())
            gv = p.dma("sync", gsl[:], gh[m * 128:(m + 1) * 128, qc * 512:(qc + 1) * 512], xls)
            pa = projacc[:, m * 512:(m + 1) * 512]
            p.q("vector", lambda e, pa=pa, m=m: e.tensor_scalar(
                pa, pa, bo_sb[:, m:m + 1], None, ALU.add, ALU.bypass))
            p.q("vector", lambda e, pa=pa: e.scalar_tensor_tensor(
                sg1[:], pa, 0.01, pa, ALU.mult, ALU.max))
            p.q("vector", lambda e, m=m, qc=qc: e.tensor_add(
                sg1[:], sg1[:], xn_sb[:, m * T + qc * 512:m * T + (qc + 1) * 512]))
            p.q("vector", lambda e: e.tensor_mul(sg1[:], sg1[:], gsl[:]),
                wait=((xls, gv),))
            p.q("vector", lambda e, b=b: e.tensor_add(x1stg[b][:], sg1[:], xsl[:]),
                wait=((xls, xv), (stx[b], 16 * (bst // 2)) if bst >= 2 else (xls, xv)))
            ex = exs.bump(1)
            p.q("vector", lambda e: e.drain(), inc=(exs, 1))
            p.dma("sync", o_x1[m * 128:(m + 1) * 128, qc * 512:(qc + 1) * 512],
                  x1stg[b][:], stx[b], wait=((exs, ex),))
        p.barrier()
    finish(p, [stx[0], stx[1]])
    return p


def build_l1a():
    p = Prog("l1a")
    x = p.inp("x", (D, T), F32)
    Wq = p.inp("Wq", (D, D), F32)
    Wf = p.inp("Wf", (D, D), F32)
    Wi = p.inp("Wi", (D, D), F32)
    o_xn = p.outp("xn1", (D, T), BF16)
    o_ol = p.outp("o_loc", (D, T), BF16)
    o_qdp = p.outp("qdp", (D, T), BF16)
    o_S = p.outp("Sout", (D, 128), F32)

    ones = p.sb("ones", (128, 128), BF16)
    ident = p.sb("ident", (128, 128), BF16)
    triu32 = p.sb("triu32", (32, 32), BF16)
    ld0 = p.sem()
    p.dma("gpsimd", ones[:], p.const(np.ones((128, 128), BF), "cones"), ld0)
    p.dma("gpsimd", ident[:], p.const(np.eye(128, dtype=BF), "cident"), ld0)
    p.dma("gpsimd", triu32[:], p.const(np.triu(np.ones((32, 32), BF)), "ctriu"), ld0)
    wq_sb = p.sb("wq_sb", (128, NCH * D), BF16)
    wf_sb = p.sb("wf_sb", (128, NCH * D), BF16)
    wi_sb = p.sb("wi_sb", (128, NCH * D), BF16)
    p.dma("gpsimd", wq_sb[:].rearrange("p (k n) -> p k n", k=NCH),
          Wq.rearrange("(k p) n -> p k n", p=128), ld0)
    p.dma("gpsimd", wf_sb[:].rearrange("p (k n) -> p k n", k=NCH),
          Wf.rearrange("(k p) n -> p k n", p=128), ld0)
    p.dma("gpsimd", wi_sb[:].rearrange("p (k n) -> p k n", k=NCH),
          Wi.rearrange("(k p) n -> p k n", p=128), ld0)
    for e in ENGINES:
        p.q(e, None, wait=((ld0, ld0.n),))

    A16 = p.sb("A16", (128, NCH * T), BF16)
    OALL = p.sb("OALL", (128, NCH * T), BF16)
    scrA = p.sb("scrA", (128, NCH * T), BF16)
    scrB = p.sb("scrB", (128, 8 * T), BF16)
    scr = ln_scratch(p, scrA, scrB, "c")
    emit_ln(p, x, A16, ones[:, 0:1], ones[0:1, :], scr)
    xn = A16
    st_xn = p.sem()
    store_fm(p, xn, o_xn, st_xn)

    qh = scrA[:, 0:T]
    kh = scrA[:, T:2 * T]
    gh = scrA[:, 2 * T:4 * T].bitcast(F32)
    eah = scrA[:, 4 * T:5 * T]
    qeh = scrA[:, 5 * T:6 * T]
    kdh = scrA[:, 6 * T:7 * T]
    fh = scrB[:, 0:2 * T].bitcast(F32)
    glh = scrB[:, 2 * T:3 * T]
    qsc = scrB[:, 3 * T:3 * T + 512].bitcast(F32)
    vth = p.sb("vth", (64, 32 * 128), BF16)
    S_f = p.sb("S_f", (128, 128), F32)
    S_b = p.sb("S_b", (128, 128), BF16)
    EC = p.sb("EC", (128, 96), F32)
    PTd = p.sb("PTd", (64, 32), BF16)
    PTp = p.sb("PTp", (32, 32), BF16)
    sqv = p.sb("sqv", (128, 64), BF16)
    skv = p.sb("skv", (128, 64), BF16)
    och = p.sb("och", (128, 64), BF16)
    kdT = p.sb("kdT", (64, 128), BF16)
    stg = [p.sem(), p.sem()]
    sto = [p.sem(), p.sem()]
    stS = p.sem()
    stqd = [p.sem(), p.sem()]

    psQ = [p.PSA[:, 0:512], p.PSA[:, 512:1024]]
    psI = p.PSA[:, 1024:1088]
    psI2 = p.PSA[:, 1536:1600]
    psD = p.PSB[:, 0:32]
    psP2 = p.PSB[:, 512:544]
    psSu = p.PSC[:, 0:128]
    psT = p.PST[:, 0:128]

    for h in range(8):
        hs = p.sem()

        def step(eng, fn, extra=()):
            v = hs.bump(1)
            p.q(eng, fn, wait=tuple(extra))
            p.q(eng, lambda e: e.drain(), inc=(hs, 1))
            for o in ("vector", "scalar", "tensor"):
                if o != eng:
                    p.q(o, None, wait=((hs, v),))

        pd = p.sem()
        for (dst, wsb, col0) in ((qh, wq_sb, h * 128), (fh, wf_sb, h * 128)):
            for ci, (c0, w) in enumerate(chunks_of(T, 512)):
                b = ci % 2
                for k in range(NCH):
                    pd.bump(1) if k == NCH - 1 else None
                    p.q("tensor",
                        lambda e, k=k, c0=c0, w=w, b=b, wsb=wsb, col0=col0: e.matmul(
                            psQ[b][0:128, 0:w], wsb[:, k * D + col0:k * D + col0 + 128],
                            xn[:, k * T + c0:k * T + c0 + w], start=(k == 0), stop=(k == NCH - 1)),
                        wait=((pd, pd.n - 2),) if (k == 0 and pd.n >= 2) else (),
                        inc=(pd, 1) if k == NCH - 1 else None)
                p.q("vector",
                    lambda e, dst=dst, c0=c0, w=w, b=b: e.tensor_copy(
                        dst[:, c0:c0 + w], psQ[b][0:128, 0:w]),
                    wait=((pd, pd.n),), inc=(pd, 1))
                pd.bump(1)
        p.barrier()
        p.q("scalar", lambda e: e.activation(out=qh, in_=qh, func=AF.Silu))
        p.q("scalar", lambda e: e.activation(out=kh, in_=fh, func=AF.Sigmoid, scale=-1.0))
        p.barrier()
        p.q("vector", lambda e: e.tensor_scalar(qh, qh, float(128 ** -0.5), None, ALU.mult, ALU.bypass))
        p.q("vector", lambda e: e.tensor_scalar(eah, fh, -1.0, None, ALU.mult, ALU.bypass))
        p.barrier()
        p.q("vector", lambda e: e.tensor_max(eah, eah, fh))
        p.barrier()
        p.q("scalar", lambda e: e.activation(out=eah, in_=eah, func=AF.Exp, scale=-1.0))
        p.barrier()
        p.q("scalar", lambda e: e.activation(out=eah, in_=eah, func=AF.Ln, bias=1.0))
        p.barrier()
        p.q("vector", lambda e: e.tensor_single_scalar(glh, fh, 0.0, ALU.min))
        p.barrier()
        p.q("vector", lambda e: e.tensor_sub(glh, glh, eah))
        p.barrier()
        vd = p.sem()
        for tt in range(32):
            b = tt % 2
            for k in range(NCH):
                p.q("tensor",
                    lambda e, k=k, tt=tt, b=b, h=h: e.matmul(
                        psQ[b][0:64, 0:128], xn[:, k * T + tt * 64:k * T + (tt + 1) * 64],
                        wi_sb[:, k * D + h * 128:k * D + (h + 1) * 128],
                        start=(k == 0), stop=(k == NCH - 1)),
                    wait=((vd, vd.n - 2),) if (k == 0 and vd.n >= 2) else (),
                    inc=(vd, 1) if k == NCH - 1 else None)
            vd.bump(1) if False else None
            vd.bump(1)
            p.q("vector",
                lambda e, tt=tt, b=b: e.tensor_copy(vth[:, tt * 128:(tt + 1) * 128],
                                                    psQ[b][0:64, 0:128]),
                wait=((vd, vd.n),), inc=(vd, 1))
            vd.bump(1)
        p.barrier()
        for ch in range(32):
            p.q("vector", lambda e, ch=ch: e.tensor_tensor_scan(
                gh[:, ch * 64:(ch + 1) * 64], glh[:, ch * 64:(ch + 1) * 64],
                glh[:, ch * 64:(ch + 1) * 64], 0.0, ALU.add, ALU.bypass))
        p.barrier()
        step("scalar", lambda e: e.activation(out=eah, in_=gh, func=AF.Exp))
        step("vector", lambda e: e.tensor_mul(qeh, qh, eah))
        step("vector", lambda e: e.tensor_copy(EC[:, 0:32].bitcast(F32) if False else EC[:, 0:32], gh[:, 63::64]))
        for ch in range(32):
            step("vector", lambda e, ch=ch: e.tensor_single_scalar(
                kdh[:, ch * 64:(ch + 1) * 64], gh[:, ch * 64:(ch + 1) * 64],
                EC[:, ch:ch + 1], ALU.subtract_rev if hasattr(ALU, "subtract_rev") else ALU.subtract))
        p.barrier()
        step("scalar", lambda e: e.activation(out=kdh, in_=kdh, func=AF.Exp, scale=-1.0))
        step("vector", lambda e: e.tensor_mul(kdh, kdh, kh))
        step("vector", lambda e: e.tensor_tensor_scan(
            EC[:, 32:64].bitcast(F32)[:, 0:32] if False else EC[:, 32:64], EC[:, 0:32], EC[:, 0:32],
            0.0, ALU.add, ALU.bypass))
        step("scalar", lambda e: e.activation(out=EC[:, 32:64], in_=EC[:, 32:64], func=AF.Exp))
        step("scalar", lambda e: e.activation(out=EC[:, 64:96], in_=EC[:, 0:32], func=AF.Exp))
        qdstg = [scrB[:, 3 * T + 1024:3 * T + 1024 + 64], scrB[:, 3 * T + 1088:3 * T + 1152]]
        for ch in range(32):
            b = ch % 2
            wv = ((stqd[b], stqd[b].n),) if stqd[b].n > 0 else ()
            if ch == 0:
                p.q("vector", lambda e, b=b: e.tensor_copy(qdstg[b], qeh[:, 0:64]), wait=wv)
            else:
                p.q("vector", lambda e, ch=ch, b=b: e.tensor_single_scalar(
                    qdstg[b], qeh[:, ch * 64:(ch + 1) * 64], EC[:, 32 + ch - 1:32 + ch], ALU.mult),
                    wait=wv)
            hv = hs.bump(1)
            p.q("vector", lambda e: e.drain(), inc=(hs, 1))
            p.dma("sync", o_qdp[h * 128:(h + 1) * 128, ch * 64:(ch + 1) * 64], qdstg[b],
                  stqd[b], wait=((hs, hv),))
        p.barrier()
        p.q("vector", lambda e: e.memset(S_f[:], 0.0))
        p.q("vector", lambda e: e.memset(S_b[:], 0.0))
        p.barrier()
        cs = p.sem()

        def cstep(eng, fn, extra=()):
            v = cs.bump(1)
            p.q(eng, fn, wait=tuple(extra))
            p.q(eng, lambda e: e.drain(), inc=(cs, 1))
            for o in ("vector", "scalar", "tensor"):
                if o != eng:
                    p.q(o, None, wait=((cs, v),))

        for ch in range(32):
            cstep("tensor", lambda e, ch=ch: e.matmul(
                psI[0:128, 0:64], S_b[:], qeh[:, ch * 64:(ch + 1) * 64],
                start=True, stop=False))
            for I in range(2):
                t0 = I * 32
                cstep("vector", lambda e, ch=ch, t0=t0: e.tensor_single_scalar(
                    sqv[:, 0:32], gh[:, ch * 64 + t0:ch * 64 + t0 + 32],
                    gh[:, ch * 64 + t0 + 16:ch * 64 + t0 + 17], ALU.subtract))
                cstep("vector", lambda e, ch=ch, t0=t0: e.tensor_single_scalar(
                    sqv[:, 32:64], gh[:, ch * 64 + t0:ch * 64 + t0 + 32],
                    gh[:, ch * 64 + t0 + 16:ch * 64 + t0 + 17], ALU.subtract))
                cstep("scalar", lambda e: e.activation(out=skv[:, 0:32], in_=sqv[:, 0:32], func=AF.Exp))
                cstep("scalar", lambda e: e.activation(out=skv[:, 32:64], in_=sqv[:, 32:64], func=AF.Exp, scale=-1.0))
                cstep("vector", lambda e, ch=ch, t0=t0: e.tensor_mul(
                    sqv[:, 0:32], skv[:, 0:32], qh[:, ch * 64 + t0:ch * 64 + t0 + 32]))
                cstep("vector", lambda e, ch=ch, t0=t0: e.tensor_mul(
                    sqv[:, 32:64], skv[:, 32:64], kh[:, ch * 64 + t0:ch * 64 + t0 + 32]))
                cstep("tensor", lambda e: e.matmul(psD[0:32, 0:32], sqv[:, 32:64], sqv[:, 0:32],
                                                   start=True, stop=True))
                cstep("vector", lambda e, t0=t0: e.tensor_mul(PTd[t0:t0 + 32, :], psD[0:32, 0:32], triu32[:]))
                cstep("tensor", lambda e, ch=ch, t0=t0: e.matmul(
                    psI[0:128, t0:t0 + 32], vth[t0:t0 + 32, ch * 128:ch * 128 + 128],
                    PTd[t0:t0 + 32, :], start=False, stop=False))
                if I == 1:
                    cstep("vector", lambda e, ch=ch: e.tensor_single_scalar(
                        sqv[:, 32:64], gh[:, ch * 64:ch * 64 + 32],
                        gh[:, ch * 64 + 31:ch * 64 + 32], ALU.subtract))
                    cstep("scalar", lambda e: e.activation(out=skv[:, 32:64], in_=sqv[:, 32:64],
                                                           func=AF.Exp, scale=-1.0))
                    cstep("vector", lambda e, ch=ch: e.tensor_single_scalar(
                        sqv[:, 0:32], gh[:, ch * 64 + 32:ch * 64 + 64],
                        gh[:, ch * 64 + 31:ch * 64 + 32], ALU.subtract))
                    cstep("scalar", lambda e: e.activation(out=skv[:, 0:32], in_=sqv[:, 0:32], func=AF.Exp))
                    cstep("vector", lambda e, ch=ch: e.tensor_mul(
                        sqv[:, 0:32], skv[:, 0:32], qh[:, ch * 64 + 32:ch * 64 + 64]))
                    cstep("vector", lambda e, ch=ch: e.tensor_mul(
                        sqv[:, 32:64], skv[:, 32:64], kh[:, ch * 64:ch * 64 + 32]))
                    cstep("tensor", lambda e: e.matmul(psP2[0:32, 0:32], sqv[:, 32:64], sqv[:, 0:32],
                                                       start=True, stop=True))
                    cstep("vector", lambda e: e.tensor_copy(PTp[:], psP2[0:32, 0:32]))
                    cstep("tensor", lambda e, ch=ch: e.matmul(
                        psI[0:128, 32:64], vth[0:32, ch * 128:ch * 128 + 128], PTp[:],
                        start=False, stop=True))
            cstep("vector", lambda e, ch=ch: e.tensor_copy(och[:, 0:64], psI[0:128, 0:64]))
            cstep("vector", lambda e, h=h, ch=ch: e.tensor_copy(
                OALL[:, h * T + ch * 64:h * T + (ch + 1) * 64], och[:, 0:64]))
            cstep("tensor", lambda e, ch=ch: e.transpose(
                p.PST[0:64, 0:128], kdh[:, ch * 64:(ch + 1) * 64], ident[:]))
            cstep("vector", lambda e: e.tensor_copy(kdT[:], p.PST[0:64, 0:128]))
            cstep("tensor", lambda e, ch=ch: e.matmul(
                psSu[0:128, 0:128], kdT[:], vth[:, ch * 128:(ch + 1) * 128],
                start=True, stop=True))
            cstep("vector", lambda e, ch=ch: e.tensor_single_scalar(
                S_f[:], S_f[:], EC[:, 64 + ch:64 + ch + 1], ALU.mult))
            cstep("vector", lambda e: e.tensor_add(S_f[:], S_f[:], psSu[0:128, 0:128]))
            cstep("vector", lambda e: e.tensor_copy(S_b[:], S_f[:]))
        ov = p.sem()
        p.q("vector", lambda e: e.drain(), inc=(ov, 1))
        ov.bump(1)
        p.dma("sync", o_ol[h * 128:(h + 1) * 128, :], OALL[:, h * T:(h + 1) * T], sto[h % 2],
              wait=((ov, ov.n),))
        p.dma("sync", o_S[h * 128:(h + 1) * 128, :], S_f[:], stS, wait=((ov, ov.n),))
        p.barrier()
    finish(p, [st_xn, sto[0], sto[1], stS, stqd[0], stqd[1]])
    return p


def build_l1b():
    p = Prog("l1b")
    xnh = p.inp("xn1", (D, T), BF16)
    gk = p.inp("gate_kernel", (D, 16), F32)
    gb = p.inp("gate_bias", (1, 16), F32)
    wg = p.inp("w_gate", (16, D, 512), F32)
    wu = p.inp("w_up", (16, D, 512), F32)
    wd = p.inp("w_down", (16, 512, D), F32)
    swg = p.inp("sw_gate", (D, D), F32)
    swu = p.inp("sw_up", (D, D), F32)
    swd = p.inp("sw_down", (D, D), F32)
    o_g = p.outp("gffn", (D, T), BF16)

    ones = p.sb("ones", (128, 128), BF16)
    ident = p.sb("ident", (128, 128), BF16)
    ld0 = p.sem()
    p.dma("gpsimd", ones[:], p.const(np.ones((128, 128), BF), "cones"), ld0)
    p.dma("gpsimd", ident[:], p.const(np.eye(128, dtype=BF), "cident"), ld0)
    xn = p.sb("xn", (128, NCH * T), BF16)
    SH1 = p.sb("SH1", (128, NCH * T), BF16)
    SH2 = p.sb("SH2", (128, NCH * T), BF16)
    YC = p.sb("YC", (128, NCH * T), BF16)
    wdb = [p.sb(f"wdb{i}", (128, 4096), BF16) for i in range(2)]
    gk_sb = p.sb("gk_sb", (128, NCH * 16), BF16)
    gbbc = p.sb("gbbc", (128, 16), F32)
    sw_g = p.sb("sw_g", (128, NCH * D), BF16)
    sw_u = p.sb("sw_u", (128, NCH * D), BF16)
    sw_d = p.sb("sw_d", (128, NCH * D), BF16)
    logits = p.sb("logits", (128, 16 * 16), F32)
    wcomb = p.sb("wcomb", (128, 16 * 16), F32)
    r1 = p.sb("r1", (128, 16), F32)
    r2 = p.sb("r2", (128, 16), F32)
    r3 = p.sb("r3", (128, 16), F32)
    c1 = p.sb("c1", (128, 4), F32)
    c2 = p.sb("c2", (128, 4), F32)
    c3 = p.sb("c3", (128, 4), F32)
    c4 = p.sb("c4", (128, 4), F32)
    m1c = p.sb("m1c", (128, 1), F32)
    gstg = [p.sb(f"gstg{i}", (128, 512), BF16) for i in range(2)]

    p.dma("gpsimd", xn[:].rearrange("p (k n) -> p k n", k=NCH),
          xnh.rearrange("(k p) n -> p k n", p=128), ld0)
    p.dma("gpsimd", gk_sb[:].rearrange("p (k n) -> p k n", k=NCH),
          gk.rearrange("(k p) n -> p k n", p=128), ld0)
    p.dma("gpsimd", sw_g[:].rearrange("p (k n) -> p k n", k=NCH),
          swg.rearrange("(k p) n -> p k n", p=128), ld0)
    p.dma("gpsimd", sw_u[:].rearrange("p (k n) -> p k n", k=NCH),
          swu.rearrange("(k p) n -> p k n", p=128), ld0)
    p.dma("gpsimd", sw_d[:].rearrange("p (k n) -> p k n", k=NCH),
          swd.rearrange("(k p) n -> p k n", p=128), ld0)
    p.dma("gpsimd", gbbc[0:1, :], gb, ld0)
    for e in ENGINES:
        p.q(e, None, wait=((ld0, ld0.n),))
    bb = p.sem()
    p.q("vector", lambda e: e.tensor_copy(SH1[0:1, 0:16], gbbc[0:1, :]))
    p.q("vector", lambda e: e.drain(), inc=(bb, 1))
    bb.bump(1)
    p.q("tensor", lambda e: e.matmul(p.PSC[:, 0:16], ones[0:1, :], SH1[0:1, 0:16],
                                     start=True, stop=True), wait=((bb, bb.n),), inc=(bb, 1))
    bb.bump(1)
    p.q("vector", lambda e: e.tensor_copy(gbbc[:], p.PSC[:, 0:16]), wait=((bb, bb.n),))
    p.barrier()

    rds = p.sem()
    rvs = p.sem()
    for t in range(16):
        for k in range(NCH):
            p.q("tensor",
                lambda e, k=k, t=t: e.matmul(p.PSC[:, 0:16],
                                             xn[:, k * T + t * 128:k * T + (t + 1) * 128],
                                             gk_sb[:, k * 16:(k + 1) * 16],
                                             start=(k == 0), stop=(k == NCH - 1)),
                wait=((rvs, rvs.n),) if (k == 0 and t > 0) else (),
                inc=(rds, 1) if k == NCH - 1 else None)
        rds.bump(1)
        rvs.bump(1)
        p.q("vector", lambda e, t=t: e.tensor_copy(logits[:, t * 16:(t + 1) * 16], p.PSC[:, 0:16]),
            wait=((rds, rds.n),), inc=(rvs, 1))
    p.barrier()
    hz = p.sem()
    for t in range(16):
        lt = logits[:, t * 16:(t + 1) * 16]
        wt = wcomb[:, t * 16:(t + 1) * 16]

        def step(eng, fn):
            v = hz.bump(1)
            p.q(eng, fn)
            p.q(eng, lambda e: e.drain(), inc=(hz, 1))
            for o in ("vector", "scalar"):
                if o != eng:
                    p.q(o, None, wait=((hz, v),))

        step("scalar", lambda e, lt=lt: e.activation(out=r1[:], in_=lt, func=AF.Sigmoid))
        step("vector", lambda e: e.tensor_add(r1[:], r1[:], gbbc[:]))
        step("vector", lambda e: e.tensor_max(c1[:], r1[:, 0::4], r1[:, 1::4]))
        step("vector", lambda e: e.tensor_max(c2[:], r1[:, 2::4], r1[:, 3::4]))
        step("vector", lambda e: e.tensor_tensor(c3[:], r1[:, 0::4], r1[:, 1::4], ALU.min))
        step("vector", lambda e: e.tensor_tensor(c4[:], r1[:, 2::4], r1[:, 3::4], ALU.min))
        step("vector", lambda e: e.tensor_tensor(c3[:], c3[:], c4[:], ALU.max))
        step("vector", lambda e: e.tensor_tensor(c4[:], c1[:], c2[:], ALU.min))
        step("vector", lambda e: e.tensor_max(c1[:], c1[:], c2[:]))
        step("vector", lambda e: e.tensor_max(c2[:], c3[:], c4[:]))
        step("vector", lambda e: e.tensor_add(c1[:], c1[:], c2[:]))
        step("vector", lambda e: e.tensor_max(c2[:, 0:1], c1[:, 0:1], c1[:, 1:2]))
        step("vector", lambda e: e.tensor_max(c2[:, 1:2], c1[:, 2:3], c1[:, 3:4]))
        step("vector", lambda e: e.tensor_tensor(c3[:, 0:1], c1[:, 0:1], c1[:, 1:2], ALU.min))
        step("vector", lambda e: e.tensor_tensor(c3[:, 1:2], c1[:, 2:3], c1[:, 3:4], ALU.min))
        step("vector", lambda e: e.tensor_tensor(c4[:, 0:1], c2[:, 0:1], c2[:, 1:2], ALU.min))
        step("vector", lambda e: e.tensor_max(c4[:, 1:2], c3[:, 0:1], c3[:, 1:2]))
        step("vector", lambda e: e.tensor_max(c4[:, 0:1], c4[:, 0:1], c4[:, 1:2]))
        step("vector", lambda e: e.tensor_single_scalar(c2[:], c1[:], c4[:, 0:1], ALU.is_ge))
        for f in range(4):
            step("vector", lambda e, f=f: e.tensor_mul(r2[:, f::4], r1[:, f::4], c2[:]))
        step("vector", lambda e: e.tensor_scalar_max(r2[:], r2[:], 0.0))
        step("vector", lambda e: e.tensor_copy(r3[:], r2[:]))
        for it in range(3):
            step("vector", lambda e: e.reduce_max(m1c[:], r3[:], axis=AX.X))
            step("vector", lambda e: e.tensor_single_scalar(r1[:], r3[:], m1c[:], ALU.is_ge))
            step("vector", lambda e: e.scalar_tensor_tensor(r3[:], r1[:], -1e30, r3[:], ALU.mult, ALU.add))
        step("vector", lambda e: e.reduce_max(m1c[:], r3[:], axis=AX.X))
        step("vector", lambda e: e.tensor_single_scalar(r1[:], r2[:], m1c[:], ALU.is_ge))
        step("vector", lambda e: e.tensor_mul(r1[:], r1[:], r2[:]))
        step("vector", lambda e: e.tensor_single_scalar(r3[:], r2[:], 0.0, ALU.is_gt))
        step("vector", lambda e, wt=wt: e.tensor_mul(wt, r1[:], r3[:]))
        step("vector", lambda e, wt=wt: e.tensor_reduce(m1c[:], wt, AX.X, ALU.add))
        step("vector", lambda e: e.tensor_scalar(m1c[:], m1c[:], 1e-20, None, ALU.add, ALU.bypass))
        step("scalar", lambda e: e.activation(out=m1c[:], in_=m1c[:], func=AF.Ln))
        step("scalar", lambda e: e.activation(out=m1c[:], in_=m1c[:], func=AF.Exp, scale=-1.0))
        step("vector", lambda e, wt=wt: e.tensor_scalar(wt, wt, m1c[:], 2.5, ALU.mult, ALU.mult))
    p.barrier()

    p.q("vector", lambda e: e.memset(YC[:], 0.0))
    p.barrier()
    wgv = [SH1[:, 0:4096], SH1[:, 4096:8192]]
    wuv = [SH1[:, 8192:12288], SH1[:, 12288:16384]]
    gev = SH2[:, 0:8192]
    inv = SH2[:, 8192:16384]
    el = [p.sem(), p.sem()]
    ef = p.sem()
    eds = [p.sem(), p.sem()]
    efr = [p.sem(), p.sem()]
    yds = [p.sem(), p.sem()]
    yfr = [p.sem(), p.sem()]
    psE = [p.PSA[:, 0:512], p.PSA[:, 512:1024]]
    psY = [p.PSA[:, 1024:1536], p.PSA[:, 1536:2048]]
    ei = [0]
    yi = [0]
    for e_ in range(16):
        b = e_ % 2
        wv = ((ef, ef.n),) if e_ >= 2 else ()
        p.dma("gpsimd", wgv[b].rearrange("p (k n) -> p k n", k=NCH),
              wg[e_].rearrange("(k p) n -> p k n", p=128), el[b], wait=wv)
        p.dma("gpsimd", wuv[b].rearrange("p (k n) -> p k n", k=NCH),
              wu[e_].rearrange("(k p) n -> p k n", p=128), el[b])
        p.dma("gpsimd", wdb[b][:].rearrange("p (k n) -> p k n", k=4),
              wd[e_].rearrange("(k p) n -> p k n", p=128), el[b])
        lv = el[b].n
        for m in range(4):
            for ci, (c0, w) in enumerate(chunks_of(T, 512)):
                i = ei[0]
                pb = i % 2
                w0 = [(efr[pb], efr[pb].n)] if i >= 2 else []
                if m == 0 and ci == 0:
                    w0.append((el[b], lv))
                for k in range(NCH):
                    p.q("tensor",
                        lambda e, k=k, m=m, c0=c0, w=w, b=b, pb=pb: e.matmul(
                            psE[pb][0:128, 0:w], wgv[b][:, k * 512 + m * 128:k * 512 + (m + 1) * 128],
                            xn[:, k * T + c0:k * T + c0 + w], start=(k == 0), stop=(k == NCH - 1)),
                        wait=tuple(w0) if k == 0 else (),
                        inc=(eds[pb], 1) if k == NCH - 1 else None)
                eds[pb].bump(1)
                efr[pb].bump(1)
                p.q("scalar",
                    lambda e, m=m, c0=c0, w=w, pb=pb: e.activation(
                        out=gev[:, m * T + c0:m * T + c0 + w], in_=psE[pb][0:128, 0:w], func=AF.Silu),
                    wait=((eds[pb], eds[pb].n),), inc=(efr[pb], 1))
                ei[0] += 1
        for m in range(4):
            for ci, (c0, w) in enumerate(chunks_of(T, 512)):
                i = ei[0]
                pb = i % 2
                w0 = [(efr[pb], efr[pb].n)] if i >= 2 else []
                for k in range(NCH):
                    p.q("tensor",
                        lambda e, k=k, m=m, c0=c0, w=w, b=b, pb=pb: e.matmul(
                            psE[pb][0:128, 0:w], wuv[b][:, k * 512 + m * 128:k * 512 + (m + 1) * 128],
                            xn[:, k * T + c0:k * T + c0 + w], start=(k == 0), stop=(k == NCH - 1)),
                        wait=tuple(w0) if k == 0 else (),
                        inc=(eds[pb], 1) if k == NCH - 1 else None)
                eds[pb].bump(1)
                efr[pb].bump(1)
                p.q("vector",
                    lambda e, m=m, c0=c0, w=w, pb=pb: e.tensor_mul(
                        inv[:, m * T + c0:m * T + c0 + w], psE[pb][0:128, 0:w],
                        gev[:, m * T + c0:m * T + c0 + w]),
                    wait=((eds[pb], eds[pb].n),), inc=(efr[pb], 1))
                ei[0] += 1
        for tt in range(16):
            for cg in range(2):
                i = yi[0]
                pb = i % 2
                w0 = [(yfr[pb], yfr[pb].n)] if i >= 2 else []
                w0.append((efr[0], efr[0].n))
                w0.append((efr[1], efr[1].n))
                for k in range(4):
                    p.q("tensor",
                        lambda e, k=k, tt=tt, cg=cg, b=b, pb=pb: e.matmul(
                            psY[pb][0:128, 0:512], inv[:, k * T + tt * 128:k * T + (tt + 1) * 128],
                            wdb[b][:, k * 1024 + cg * 512:k * 1024 + (cg + 1) * 512],
                            start=(k == 0), stop=(k == 3)),
                        wait=tuple(w0) if k == 0 else (),
                        inc=(yds[pb], 1) if k == 3 else None)
                yds[pb].bump(1)
                yfr[pb].bump(1)
                p.q("vector",
                    lambda e, tt=tt, cg=cg, e_=e_, pb=pb: e.scalar_tensor_tensor(
                        YC[:, tt * 1024 + cg * 512:tt * 1024 + (cg + 1) * 512],
                        psY[pb][0:128, 0:512], wcomb[:, tt * 16 + e_:tt * 16 + e_ + 1],
                        YC[:, tt * 1024 + cg * 512:tt * 1024 + (cg + 1) * 512],
                        ALU.mult, ALU.add),
                    wait=((yds[pb], yds[pb].n),), inc=(yfr[pb], 1))
                yi[0] += 1
        ef.bump(1)
        p.q("tensor", lambda e: e.nop(), inc=(ef, 1))
    p.barrier()

    ycfm = SH2
    trs = p.sem()
    cpd = p.sem()
    for t in range(16):
        for dc in range(NCH):
            last = (dc == NCH - 1)
            if last:
                trs.bump(1)
            p.q("tensor",
                lambda e, t=t, dc=dc: e.transpose(
                    p.PST[:, dc * 128:(dc + 1) * 128],
                    YC[:, t * 1024 + dc * 128:t * 1024 + (dc + 1) * 128], ident[:]),
                wait=((cpd, t),) if (dc == 0 and t >= 1) else (),
                inc=(trs, 1) if last else None)
        cpd.bump(1)
        p.q("vector",
            lambda e, t=t: e.tensor_copy(
                ycfm[:].rearrange("p (dc tt) -> p dc tt", dc=NCH)[:, :, t * 128:(t + 1) * 128],
                p.PST[:].rearrange("p (dc tt) -> p dc tt", dc=NCH)),
            wait=((trs, trs.n),), inc=(cpd, 1))
    p.barrier()

    gsh = SH1
    ush = YC
    dn = [p.sem(), p.sem()]
    fr = [p.sem(), p.sem()]
    si = [0]

    def mlp_pass(wsb, dst, act):
        for m in range(NCH):
            for (c0, w) in chunks_of(T, 512):
                i = si[0]
                pb = i % 2
                w0 = [(fr[pb], fr[pb].n)] if i >= 2 else []
                for k in range(NCH):
                    p.q("tensor",
                        lambda e, k=k, m=m, c0=c0, w=w, pb=pb, wsb=wsb: e.matmul(
                            psE[pb][0:128, 0:w], wsb[:, k * D + m * 128:k * D + (m + 1) * 128],
                            xn[:, k * T + c0:k * T + c0 + w], start=(k == 0), stop=(k == NCH - 1)),
                        wait=tuple(w0) if k == 0 else (),
                        inc=(dn[pb], 1) if k == NCH - 1 else None)
                dn[pb].bump(1)
                fr[pb].bump(1)
                if act == "silu":
                    p.q("scalar",
                        lambda e, m=m, c0=c0, w=w, pb=pb, dst=dst: e.activation(
                            out=dst[:, m * T + c0:m * T + c0 + w], in_=psE[pb][0:128, 0:w], func=AF.Silu),
                        wait=((dn[pb], dn[pb].n),), inc=(fr[pb], 1))
                else:
                    p.q("vector",
                        lambda e, m=m, c0=c0, w=w, pb=pb, dst=dst: e.tensor_mul(
                            dst[:, m * T + c0:m * T + c0 + w], psE[pb][0:128, 0:w],
                            gsh[:, m * T + c0:m * T + c0 + w]),
                        wait=((dn[pb], dn[pb].n),), inc=(fr[pb], 1))
                si[0] += 1

    mlp_pass(sw_g, gsh, "silu")
    p.barrier()
    mlp_pass(sw_u, ush, "mul")
    p.barrier()
    stg = [p.sem(), p.sem()]
    evg = p.sem()
    gi = [0]
    for m in range(NCH):
        for (c0, w) in chunks_of(T, 512):
            i = gi[0]
            pb = i % 2
            w0 = [(fr[pb], fr[pb].n)] if True else []
            for k in range(NCH):
                p.q("tensor",
                    lambda e, k=k, m=m, c0=c0, w=w, pb=pb: e.matmul(
                        psE[pb][0:128, 0:w], sw_d[:, k * D + m * 128:k * D + (m + 1) * 128],
                        ush[:, k * T + c0:k * T + c0 + w], start=(k == 0), stop=(k == NCH - 1)),
                    wait=tuple(w0) if k == 0 else (),
                    inc=(dn[pb], 1) if k == NCH - 1 else None)
            dn[pb].bump(1)
            fr[pb].bump(1)
            bs = i % 2
            wv = ((dn[pb], dn[pb].n),)
            if i >= 2:
                wv = wv + ((stg[bs], 16 * (i // 2)),)
            av = evg.bump(1)
            p.q("vector",
                lambda e, m=m, c0=c0, w=w, pb=pb: e.tensor_add(
                    psE[pb][0:128, 0:w], psE[pb][0:128, 0:w],
                    ycfm[:, m * T + c0:m * T + c0 + w]),
                wait=wv)
            p.q("vector", lambda e: e.drain(), inc=(evg, 1))
            ev = evg.bump(1)
            p.q("scalar",
                lambda e, pb=pb, bs=bs, w=w: e.activation(out=gstg[bs][:, 0:w],
                                                          in_=psE[pb][0:128, 0:w], func=AF.Silu),
                wait=((evg, av),), inc=(fr[pb], 1))
            p.q("scalar", lambda e: e.drain(), inc=(evg, 1))
            p.dma("sync", o_g[m * 128:(m + 1) * 128, c0:c0 + w], gstg[bs][:, 0:w], stg[bs],
                  wait=((evg, ev),))
            gi[0] += 1
    finish(p, [stg[0], stg[1]])
    return p


def build_l1c():
    p = Prog("l1c")
    olh = p.inp("o_loc", (D, T), BF16)
    qdph = p.inp("qdp", (D, T), BF16)
    Sin = p.inp("Sin", (D, 128), F32)
    gwT = p.inp("gwT", (128, 8), F32)
    Wo = p.inp("Wo", (D, D), F32)
    gffnh = p.inp("gffn", (D, T), BF16)
    x1h = p.inp("x1", (D, T), F32)
    o_x2 = p.outp("x2", (D, T), F32)

    ones = p.sb("ones", (128, 128), BF16)
    ld0 = p.sem()
    p.dma("gpsimd", ones[:], p.const(np.ones((128, 128), BF), "cones"), ld0)
    OA = p.sb("OA", (128, NCH * T), BF16)
    QD = p.sb("QD", (128, NCH * T), BF16)
    GF = p.sb("GF", (128, NCH * T), BF16)
    SN = p.sb("SN", (128, NCH * 128), BF16)
    WO = p.sb("WO", (128, NCH * D), BF16)
    GW = p.sb("GW", (128, 8), F32)
    SQ = p.sb("SQ", (128, 512), BF16)
    msr = p.sb("msr", (1, T), F32)
    msb = p.sb("msb", (1, T), BF16)
    rbc = p.sb("rbc", (128, T), BF16)
    xsl = p.sb("xsl", (128, 512), F32)
    x2stg = [p.sb(f"x2stg{i}", (128, 512), F32) for i in range(2)]
    p.dma("gpsimd", OA[:].rearrange("p (k n) -> p k n", k=NCH),
          olh.rearrange("(k p) n -> p k n", p=128), ld0)
    p.dma("gpsimd", QD[:].rearrange("p (k n) -> p k n", k=NCH),
          qdph.rearrange("(k p) n -> p k n", p=128), ld0)
    p.dma("gpsimd", GF[:].rearrange("p (k n) -> p k n", k=NCH),
          gffnh.rearrange("(k p) n -> p k n", p=128), ld0)
    p.dma("gpsimd", SN[:].rearrange("p (k n) -> p k n", k=NCH),
          Sin.rearrange("(k p) n -> p k n", p=128), ld0)
    p.dma("gpsimd", WO[:].rearrange("p (k n) -> p k n", k=NCH),
          Wo.rearrange("(k p) n -> p k n", p=128), ld0)
    p.dma("gpsimd", GW[:], gwT, ld0)
    for e in ENGINES:
        p.q(e, None, wait=((ld0, ld0.n),))

    cd = p.sem()
    cv = p.sem()
    for h in range(8):
        for ci, (c0, w) in enumerate(chunks_of(T, 512)):
            i = h * 4 + ci
            pb = i % 2
            ps = [p.PSA[:, 0:512], p.PSA[:, 512:1024]][pb]
            cd.bump(1)
            p.q("tensor",
                lambda e, h=h, c0=c0, w=w, ps=ps: e.matmul(
                    ps[0:128, 0:w], SN[:, h * 128:(h + 1) * 128],
                    QD[:, h * T + c0:h * T + c0 + w], start=True, stop=True),
                wait=((cv, cv.n - 1),) if i >= 2 else (), inc=(cd, 1))
            cv.bump(1)
            p.q("vector",
                lambda e, h=h, c0=c0, w=w, ps=ps: e.tensor_add(
                    OA[:, h * T + c0:h * T + c0 + w], OA[:, h * T + c0:h * T + c0 + w],
                    ps[0:128, 0:w]),
                wait=((cd, cd.n),), inc=(cv, 1))
    p.barrier()
    sq = p.sem()
    for gi_, (c0, w) in enumerate(chunks_of(T, 512)):
        for c in range(NCH):
            v = sq.bump(1)
            p.q("scalar", lambda e, c=c, c0=c0, w=w: e.activation(
                out=SQ[:, 0:w], in_=OA[:, c * T + c0:c * T + c0 + w], func=AF.Square),
                wait=((sq, sq.n - 2),) if sq.n >= 3 else ())
            p.q("scalar", lambda e: e.drain(), inc=(sq, 1))
            v2 = sq.bump(1)
            p.q("tensor", lambda e, c=c, w=w: e.matmul(
                p.PSC[0:1, 0:w], ones[:, 0:1], SQ[:, 0:w],
                start=(c == 0), stop=(c == NCH - 1)),
                wait=((sq, v),), inc=(sq, 1))
        ev = sq.bump(1)
        p.q("vector", lambda e, c0=c0, w=w: e.tensor_copy(msr[:, c0:c0 + w], p.PSC[0:1, 0:w]),
            wait=((sq, sq.n - 1),))
        p.q("vector", lambda e: e.drain(), inc=(sq, 1))
        p.q("tensor", None, wait=((sq, ev),))
    p.barrier()
    p.q("vector", lambda e: e.tensor_scalar(msr[:], msr[:], 1.0 / D, 1e-5, ALU.mult, ALU.add))
    p.barrier()
    p.q("scalar", lambda e: e.activation(out=msr[:], in_=msr[:], func=AF.Ln))
    p.barrier()
    p.q("scalar", lambda e: e.activation(out=msb[:], in_=msr[:], func=AF.Exp, scale=-0.5))
    p.barrier()
    bb = p.sem()
    for gi_, (c0, w) in enumerate(chunks_of(T, 512)):
        v = bb.bump(1)
        p.q("tensor", lambda e, c0=c0, w=w: e.matmul(
            p.PSC[:, 0:w], ones[0:1, :], msb[:, c0:c0 + w], start=True, stop=True),
            wait=((bb, bb.n - 1),) if bb.n >= 2 else (), inc=(bb, 1))
        bb.bump(1)
        p.q("vector", lambda e, c0=c0, w=w: e.tensor_copy(rbc[:, c0:c0 + w], p.PSC[:, 0:w]),
            wait=((bb, v),), inc=(bb, 1))
    p.barrier()
    for c in range(NCH):
        p.q("vector", lambda e, c=c: e.tensor_mul(OA[:, c * T:(c + 1) * T],
                                                  OA[:, c * T:(c + 1) * T], rbc[:]))
    p.barrier()
    for c in range(NCH):
        p.q("vector", lambda e, c=c: e.tensor_single_scalar(
            OA[:, c * T:(c + 1) * T], OA[:, c * T:(c + 1) * T], GW[:, c:c + 1], ALU.mult))
    p.barrier()
    stx = [p.sem(), p.sem()]
    xls = p.sem()
    exs = p.sem()
    sti = [0]

    def evac_s(ps, m, mr, c0, w, wait, freeinc):
        i = sti[0]
        b = i % 2
        xv = p.dma("sync", xsl[:, 0:w], x1h[m * 128:(m + 1) * 128, c0:c0 + w], xls,
                   wait=((exs, exs.n),) if i >= 1 else ())
        p.q("vector",
            lambda e, ps=ps, m=m, c0=c0, w=w: e.tensor_mul(
                ps[0:128, 0:w], ps[0:128, 0:w], GF[:, m * T + c0:m * T + c0 + w]),
            wait=wait)
        wv = ((xls, xv),)
        if i >= 2:
            wv = wv + ((stx[b], 16 * (i // 2)),)
        p.q("vector",
            lambda e, ps=ps, b=b, w=w: e.tensor_add(x2stg[b][:, 0:w], ps[0:128, 0:w], xsl[:, 0:w]),
            wait=wv, inc=freeinc)
        ev = exs.bump(1)
        p.q("vector", lambda e: e.drain(), inc=(exs, 1))
        p.dma("sync", o_x2[m * 128:(m + 1) * 128, c0:c0 + w], x2stg[b][:, 0:w], stx[b],
              wait=((exs, ev),))
        sti[0] += 1

    mm_tiled(p, nK=NCH, m_list=[(m, 128) for m in range(NCH)],
             col_list=chunks_of(T, 512),
             lhsT=lambda k, m: WO[:, k * D + m * 128:k * D + (m + 1) * 128],
             rhs=lambda k, c0, w: OA[:, k * T + c0:k * T + c0 + w],
             evac=evac_s, pss=[p.PSA[:, 0:512], p.PSA[:, 512:1024]])
    finish(p, [stx[0], stx[1]])
    return p


def build_l2a():
    p = Prog("l2a")
    TE = T + 64
    x = p.inp("x2f", (D, TE), F32)
    gWT = p.inp("gWT", (D, 9216), F32)
    vWT = p.inp("vWT", (D, D), F32)
    gamT = p.inp("gamT", (128, 8), F32)
    betT = p.inp("betT", (128, 8), F32)
    o_s2 = p.outp("s2", (D, T), BF16)
    o_xn2 = p.outp("xn2", (D, T), BF16)

    SHIFTS = [0, -32, -16, -4, -1, 1, 4, 16, 32]
    ones = p.sb("ones", (128, 128), BF16)
    ld0 = p.sem()
    p.dma("gpsimd", ones[:], p.const(np.ones((128, 128), BF), "cones"), ld0)
    XN = p.sb("XN", (128, NCH * TE), BF16)
    VB = p.sb("VB", (128, NCH * TE), BF16)
    ACC = p.sb("ACC", (128, NCH * T), BF16)
    scrA = p.sb("scrA", (128, NCH * TE), BF16)
    scrB = p.sb("scrB", (128, NCH * TE), BF16)
    vw_sb = p.sb("vw_sb", (128, NCH * D), BF16)
    gw_b = [p.sb(f"gw_b{i}", (128, NCH * 128), BF16) for i in range(2)]
    gamb = p.sb("gamb", (128, 8), F32)
    betb = p.sb("betb", (128, 8), F32)
    gat = p.sb("gat", (128, 512), BF16)
    tm = p.sb("tm", (128, 512), BF16)
    s2stg = [p.sb(f"s2stg{i}", (128, 512), BF16) for i in range(2)]
    p.dma("gpsimd", vw_sb[:].rearrange("p (k n) -> p k n", k=NCH),
          vWT.rearrange("(k p) n -> p k n", p=128), ld0)
    p.dma("gpsimd", gamb[:], gamT, ld0)
    p.dma("gpsimd", betb[:], betT, ld0)
    for e in ENGINES:
        p.q(e, None, wait=((ld0, ld0.n),))

    scr = ln_scratch(p, scrA, scrB, "d", Tn=TE)
    emit_ln(p, x, XN, ones[:, 0:1], ones[0:1, :], scr, Tn=TE)
    st_xn = p.sem()
    for c in range(NCH):
        p.dma("sync", o_xn2[c * 128:(c + 1) * 128, :], XN[:, c * TE + 32:c * TE + 32 + T], st_xn)

    mm_tiled(p, nK=NCH, m_list=[(m, 128) for m in range(NCH)],
             col_list=chunks_of(TE, 512),
             lhsT=lambda k, m: vw_sb[:, k * D + m * 128:k * D + (m + 1) * 128],
             rhs=lambda k, c0, w: XN[:, k * TE + c0:k * TE + c0 + w],
             evac=lambda ps, m, mr, c0, w, wait, freeinc: p.q(
                 "vector", lambda e, ps=ps, m=m, c0=c0, w=w: e.tensor_copy(
                     VB[:, m * TE + c0:m * TE + c0 + w], ps[0:128, 0:w]),
                 wait=wait, inc=freeinc),
             pss=[p.PSA[:, 0:512], p.PSA[:, 512:1024]])
    p.barrier()
    p.q("vector", lambda e: e.memset(ACC[:], 0.0))
    p.barrier()
    gl = [p.sem(), p.sem()]
    gf = p.sem()
    gds = [p.sem(), p.sem()]
    gfr = [p.sem(), p.sem()]
    av = p.sem()
    gmu = p.sem()
    psG = [p.PSA[:, 0:512], p.PSA[:, 512:1024]]
    gi = 0
    for n in range(9):
        s_ = SHIFTS[n]
        for m in range(NCH):
            nm = n * NCH + m
            b = nm % 2
            p.dma("gpsimd", gw_b[b][:].rearrange("p (k c) -> p k c", k=NCH),
                  gWT[:, n * D + m * 128:n * D + (m + 1) * 128].rearrange("(k p) c -> p k c", p=128),
                  gl[b], wait=((gf, gf.n - 1),) if nm >= 2 else ())
            for ci, (c0, w) in enumerate(chunks_of(T, 512)):
                i = gi + ci
                pb = i % 2
                w0 = [(gfr[pb], gfr[pb].n)] if i >= 2 else []
                if ci == 0:
                    w0.append((gl[b], gl[b].n))
                for k in range(NCH):
                    p.q("tensor",
                        lambda e, k=k, c0=c0, w=w, b=b, pb=pb: e.matmul(
                            psG[pb][0:128, 0:w], gw_b[b][:, k * 128:(k + 1) * 128],
                            XN[:, k * TE + 32 + c0:k * TE + 32 + c0 + w],
                            start=(k == 0), stop=(k == NCH - 1)),
                        wait=tuple(w0) if k == 0 else (),
                        inc=(gds[pb], 1) if k == NCH - 1 else None)
                gds[pb].bump(1)
                gv = av.bump(1)
                p.q("scalar",
                    lambda e, pb=pb, w=w: e.activation(out=gat[:, 0:w], in_=psG[pb][0:128, 0:w],
                                                       func=AF.Sigmoid),
                    wait=((gds[pb], gds[pb].n), (gmu, gmu.n)), inc=(av, 1))
                gfr[pb].bump(1)
                gmu.bump(1)
                p.q("vector",
                    lambda e, m=m, c0=c0, w=w, s_=s_: e.tensor_mul(
                        tm[:, 0:w], gat[:, 0:w], VB[:, m * TE + 32 + c0 + s_:m * TE + 32 + c0 + s_ + w]),
                    wait=((av, gv),), inc=(gfr[pb], 1))
                p.q("vector", lambda e: e.drain(), inc=(gmu, 1))
                p.q("vector",
                    lambda e, m=m, c0=c0, w=w: e.tensor_add(
                        ACC[:, m * T + c0:m * T + c0 + w], ACC[:, m * T + c0:m * T + c0 + w],
                        tm[:, 0:w]))
            gf.bump(1)
            p.q("tensor", lambda e: e.nop(), inc=(gf, 1))
            gi += 4
    p.barrier()
    sq = p.sem()
    mu = scr["mu"]; e2 = scr["e2"]; r_ = scr["r_"]; mr = scr["mr"]
    rb = scr["rb"]; mrb = scr["mrb"]; rbc = scr["rbc"]; mrbc = scr["mrbc"]
    sqb = scr["sqb"]
    psx = p.PSA[0:1, 0:512]
    psq = p.PSA[0:1, 512:1024]
    evs = p.sem()
    tds = p.sem()
    sqs = p.sem()
    for gi_, (c0, w) in enumerate(chunks_of(T, 512)):
        for c in range(NCH):
            b = c % 2
            sv = sqs.bump(1)
            p.q("scalar",
                lambda e, b=b, c=c, c0=c0, w=w: e.activation(
                    out=sqb[b][:, 0:w], in_=ACC[:, c * T + c0:c * T + c0 + w], func=AF.Square),
                wait=((tds, tds.n - 1),) if sv > 2 else (), inc=(sqs, 1))
            p.q("tensor",
                lambda e, c=c, c0=c0, w=w: e.matmul(
                    psx[0:1, 0:w], ones[:, 0:1], ACC[:, c * T + c0:c * T + c0 + w],
                    start=(c == 0), stop=(c == NCH - 1)),
                wait=((evs, evs.n),) if (c == 0 and gi_ > 0) else ())
            tds.bump(1)
            p.q("tensor",
                lambda e, b=b, c=c, w=w: e.matmul(psq[0:1, 0:w], ones[:, 0:1], sqb[b][:, 0:w],
                                                  start=(c == 0), stop=(c == NCH - 1)),
                wait=((sqs, sv),) + (((evs, evs.n),) if (c == 0 and gi_ > 0) else ()),
                inc=(tds, 1))
        dv = tds.n
        ev = evs.bump(1)
        p.q("vector", lambda e, c0=c0, w=w: e.tensor_copy(mu[:, c0:c0 + w], psx[0:1, 0:w]),
            wait=((tds, dv),))
        p.q("vector", lambda e, c0=c0, w=w: e.tensor_copy(e2[:, c0:c0 + w], psq[0:1, 0:w]),
            inc=(evs, 1))
    p.barrier()
    p.q("vector", lambda e: e.tensor_scalar(mu[:, 0:T], mu[:, 0:T], 1.0 / D, None, ALU.mult, ALU.bypass))
    p.q("vector", lambda e: e.tensor_scalar(e2[:, 0:T], e2[:, 0:T], 1.0 / D, None, ALU.mult, ALU.bypass))
    p.q("vector", lambda e: e.tensor_mul(r_[:, 0:T], mu[:, 0:T], mu[:, 0:T]))
    p.q("vector", lambda e: e.tensor_sub(e2[:, 0:T], e2[:, 0:T], r_[:, 0:T]))
    p.q("vector", lambda e: e.tensor_scalar(e2[:, 0:T], e2[:, 0:T], 1e-5, None, ALU.add, ALU.bypass))
    p.barrier()
    p.q("scalar", lambda e: e.activation(out=r_[:, 0:T], in_=e2[:, 0:T], func=AF.Ln))
    p.barrier()
    p.q("scalar", lambda e: e.activation(out=r_[:, 0:T], in_=r_[:, 0:T], func=AF.Exp, scale=-0.5))
    p.barrier()
    p.q("vector", lambda e: e.tensor_mul(mr[:, 0:T], mu[:, 0:T], r_[:, 0:T]))
    p.q("vector", lambda e: e.tensor_copy(rb[:, 0:T], r_[:, 0:T]))
    p.q("vector", lambda e: e.tensor_copy(mrb[:, 0:T], mr[:, 0:T]))
    p.barrier()
    bmm = p.sem()
    bcp = p.sem()
    for (src, dst) in ((rb, rbc), (mrb, mrbc)):
        for (c0, w) in chunks_of(T, 512):
            mv = bmm.bump(1)
            p.q("tensor", lambda e, src=src, c0=c0, w=w: e.matmul(
                p.PSC[:, 0:w], ones[0:1, :], src[:, c0:c0 + w], start=True, stop=True),
                wait=((bcp, bcp.n),) if bcp.n > 0 else (), inc=(bmm, 1))
            bcp.bump(1)
            p.q("vector", lambda e, dst=dst, c0=c0, w=w: e.tensor_copy(
                dst[:, c0:c0 + w], p.PSC[:, 0:w]), wait=((bmm, mv),), inc=(bcp, 1))
    p.barrier()
    sts = [p.sem(), p.sem()]
    exs = p.sem()
    sti = 0
    for c in range(NCH):
        for (c0, w) in chunks_of(T, 512):
            b = sti % 2
            wv = ((sts[b], 16 * (sti // 2)),) if sti >= 2 else ()
            p.q("vector", lambda e, c=c, c0=c0, w=w, b=b: e.tensor_mul(
                s2stg[b][:, 0:w], ACC[:, c * T + c0:c * T + c0 + w], rbc[:, c0:c0 + w]),
                wait=wv)
            p.q("vector", lambda e, c0=c0, w=w, b=b: e.tensor_sub(
                s2stg[b][:, 0:w], s2stg[b][:, 0:w], mrbc[:, c0:c0 + w]))
            p.q("vector", lambda e, c=c, w=w, b=b: e.tensor_scalar(
                s2stg[b][:, 0:w], s2stg[b][:, 0:w], gamb[:, c:c + 1], betb[:, c:c + 1],
                ALU.mult, ALU.add))
            ev = exs.bump(1)
            p.q("vector", lambda e: e.drain(), inc=(exs, 1))
            p.dma("sync", o_s2[c * 128:(c + 1) * 128, c0:c0 + w], s2stg[b][:, 0:w], sts[b],
                  wait=((exs, ev),))
            sti += 1
    finish(p, [st_xn, sts[0], sts[1]])
    return p


TE_X = 2560
TE_Q = 2304
TE_K = 2560
TE_F = 2112
XP0 = 576
TE_P = 2112


def build_l2f(debug=False):
    p = Prog("l2f")
    x = p.inp("x", (D, TE_X), F32)
    Wqkv = p.inp("Wqkv", (D, 3072), F32)
    Wo = p.inp("Wo", (1024, D), F32)
    qsT = p.inp("qsT", (128, 1), F32)
    ksT = p.inp("ksT", (128, 1), F32)
    CQ = p.inp("CQ", (128, TE_Q), F32)
    SQ = p.inp("SQ", (128, TE_Q), F32)
    CK = p.inp("CK", (128, TE_K), F32)
    SK = p.inp("SK", (128, TE_K), F32)
    MKI = p.inp("MKI", (128, 6 * 512), F32)
    lngT = p.inp("lngT", (128, 8), F32)
    lnbT = p.inp("lnbT", (128, 8), F32)
    W1 = p.inp("W1", (D, 5460), F32)
    b1pk = p.inp("b1pk", (128, 44), F32)
    W2 = p.inp("W2", (2730, D), F32)
    b2T = p.inp("b2T", (128, 8), F32)
    o_x = p.outp("xout", (D, T), F32)

    ones = p.sb("ones", (128, 128), BF16)
    ident = p.sb("ident", (128, 128), BF16)
    pm = np.zeros((128, 128), np.float32)
    for blk in range(2):
        for d_ in range(32):
            pm[blk * 64 + d_ + 32, blk * 64 + d_] = -1.0
            pm[blk * 64 + d_, blk * 64 + d_ + 32] = 1.0
    pswap = p.sb("pswap", (128, 128), BF16)
    ld0 = p.sem()
    p.dma("gpsimd", ones[:], p.const(np.ones((128, 128), BF), "cones"), ld0)
    p.dma("gpsimd", ident[:], p.const(np.eye(128, dtype=BF), "cident"), ld0)
    p.dma("gpsimd", pswap[:], p.const(pm.astype(BF), "cpswap"), ld0)

    XB = p.sb("XB", (128, NCH * TE_X), BF16)
    QP = p.sb("QP", (128, NCH * TE_Q), BF16)
    KP = p.sb("KP", (128, NCH * TE_K), BF16)
    VT = p.sb("VT", (128, 20 * 1024), BF16)
    XPB = p.sb("XPB", (128, NCH * TE_P), BF16)
    wvt = [XPB[:, 0:4096], XPB[:, 4096:8192]]
    wst = [XPB[:, 9728:10752], XPB[:, 10752:11776]]
    CQb = XPB[:, 0:TE_Q]
    SQb = XPB[:, TE_Q:2 * TE_Q]
    CKb = XPB[:, 2 * TE_Q:2 * TE_Q + TE_K]
    SKb = XPB[:, 2 * TE_Q + TE_K:2 * TE_Q + 2 * TE_K]
    MK = p.sb("MK", (128, 6 * 512), BF16)
    w2t = [MK[:, 0:512], MK[:, 512:1024]]
    wo_sb = p.sb("wo_sb", (128, 2048), BF16)
    qscol = p.sb("qscol", (128, 1), F32)
    kscol = p.sb("kscol", (128, 1), F32)
    lngb = p.sb("lngb", (128, 8), F32)
    lnbb = p.sb("lnbb", (128, 8), F32)
    b1b = p.sb("b1b", (128, 44), F32)
    b2b = p.sb("b2b", (128, 8), F32)
    QSCR = XPB[:, 11776:14336]
    SQ_sb = QSCR[:, 0:512]
    RBC = QSCR[:, 512:1024]
    QN = QSCR[:, 1024:1536]
    TT = QSCR[:, 1536:2048]
    TU = QSCR[:, 2048:2560]
    xstg = [MK[:, 1024:2048].bitcast(F32), MK[:, 2048:3072].bitcast(F32)]
    nmc = p.sb("nmc", (128, 1), F32)
    den = p.sb("den", (128, 1), F32)
    PT = p.sb("PT", (128, 512), BF16)
    PTT = p.sb("PTT", (128, 512), BF16)
    AOT = p.sb("AOT", (128, 8 * 128), BF16)
    SS2dbg = MK[:, 1024:2048].bitcast(F32)
    SS3dbg = MK[:, 1024:3072].bitcast(F32)
    nr = XPB[0:1, 14336:15360].bitcast(F32)
    nrb = XPB[0:1, 15360:15872]

    p.dma("gpsimd", XB[:].rearrange("p (k n) -> p k n", k=NCH),
          x.rearrange("(k p) n -> p k n", p=128), ld0)
    for (dst, src) in ((CQb, CQ), (SQb, SQ), (CKb, CK), (SKb, SK), (MK[:], MKI)):
        p.dma("gpsimd", dst, src, ld0)
    for (dst, src) in ((qscol, qsT), (kscol, ksT), (lngb, lngT), (lnbb, lnbT),
                      (b1b, b1pk), (b2b, b2T)):
        p.dma("gpsimd", dst[:], src, ld0)
    for e in ENGINES:
        p.q(e, None, wait=((ld0, ld0.n),))

    wl = [p.sem(), p.sem()]
    wfr = p.sem()
    qs_ = p.sem()
    wi_ = 0
    for m in range(16):
        isq = (m < 8)
        b = m % 2
        p.dma("gpsimd", wst[b][:].rearrange("p (k c) -> p k c", k=NCH),
              Wqkv[:, m * 128:(m + 1) * 128].rearrange("(k p) c -> p k c", p=128),
              wl[b], wait=((wfr, wfr.n - 1),) if m >= 2 else ())
        hp = m % 8
        Tq = TE_Q if isq else TE_K
        x0 = 256 if isq else 0
        DSTC, DSTS = (CQb, SQb) if isq else (CKb, SKb)
        DSTP = QP if isq else KP
        scol = qscol if isq else kscol

        def qstep(eng, fn, extra=()):
            v = qs_.bump(1)
            p.q(eng, fn, wait=tuple(extra))
            p.q(eng, lambda e: e.drain(), inc=(qs_, 1))
            for o in ("vector", "scalar", "tensor"):
                if o != eng:
                    p.q(o, None, wait=((qs_, v),))

        for ci, (c0, w) in enumerate(chunks_of(Tq, 512)):
            psq = [p.PSA[:, 0:512], p.PSA[:, 512:1024]][wi_ % 2]
            w0 = []
            if ci == 0:
                w0.append((wl[b], wl[b].n))
            for k in range(NCH):
                sh = -1 if k >= 4 else 0
                qs_.bump(1) if False else None
                p.q("tensor",
                    lambda e, k=k, c0=c0, w=w, b=b, sh=sh, x0=x0, psq=psq: e.matmul(
                        psq[0:128, 0:w], wst[b][:, k * 128:(k + 1) * 128],
                        XB[:, k * TE_X + x0 + c0 + sh:k * TE_X + x0 + c0 + sh + w],
                        start=(k == 0), stop=(k == NCH - 1)),
                    wait=tuple(w0) if k == 0 else ())
            qstep("tensor", lambda e: e.nop())
            qstep("scalar", lambda e, psq=psq, w=w: e.activation(out=SQ_sb[:, 0:w],
                                                                 in_=psq[0:128, 0:w], func=AF.Square))
            qstep("tensor", lambda e, w=w: e.matmul(p.PSA[0:1, 1024:1024 + w], ones[0:64, 0:1],
                                                    SQ_sb[0:64, 0:w], start=True, stop=True))
            qstep("tensor", lambda e, w=w: e.matmul(p.PSA[0:1, 1536:1536 + w], ones[64:128, 0:1],
                                                    SQ_sb[64:128, 0:w], start=True, stop=True))
            qstep("vector", lambda e, w=w: e.tensor_scalar(nr[:, 0:w], p.PSA[0:1, 1024:1024 + w],
                                                           1e-24, None, ALU.add, ALU.bypass))
            qstep("scalar", lambda e, w=w: e.activation(out=nr[:, 0:w], in_=nr[:, 0:w], func=AF.Ln))
            qstep("scalar", lambda e, w=w: e.activation(out=nrb[:, 0:w], in_=nr[:, 0:w],
                                                        func=AF.Exp, scale=-0.5))
            qstep("tensor", lambda e, w=w: e.matmul(p.PSC[0:64, 0:w], ones[0:1, 0:64],
                                                    nrb[:, 0:w], start=True, stop=True))
            qstep("vector", lambda e, w=w: e.tensor_scalar(nr[:, 0:w], p.PSA[0:1, 1536:1536 + w],
                                                           1e-24, None, ALU.add, ALU.bypass))
            qstep("scalar", lambda e, w=w: e.activation(out=nr[:, 0:w], in_=nr[:, 0:w], func=AF.Ln))
            qstep("scalar", lambda e, w=w: e.activation(out=nrb[:, 0:w], in_=nr[:, 0:w],
                                                        func=AF.Exp, scale=-0.5))
            qstep("tensor", lambda e, w=w: e.matmul(p.PSC[64:128, 0:w], ones[0:1, 0:64],
                                                    nrb[:, 0:w], start=True, stop=True))
            qstep("vector", lambda e, w=w: e.tensor_copy(RBC[:, 0:w], p.PSC[0:128, 0:w]))
            qstep("vector", lambda e, w=w, psq=psq: e.tensor_mul(QN[:, 0:w], psq[0:128, 0:w],
                                                                 RBC[:, 0:w]))
            qstep("vector", lambda e, w=w, scol=scol: e.tensor_single_scalar(
                QN[:, 0:w], QN[:, 0:w], scol[:], ALU.mult))
            qstep("tensor", lambda e, w=w: e.matmul(p.PSA[:, 1024:1024 + w], pswap[:],
                                                    QN[:, 0:w], start=True, stop=True))
            qstep("vector", lambda e, w=w, c0=c0, DSTC=DSTC: e.tensor_mul(
                TT[:, 0:w], QN[:, 0:w], DSTC[:, c0:c0 + w]))
            qstep("vector", lambda e, w=w, c0=c0, DSTS=DSTS: e.tensor_mul(
                TU[:, 0:w], p.PSA[:, 1024:1024 + w], DSTS[:, c0:c0 + w]))
            qstep("vector", lambda e, w=w, c0=c0, hp=hp, DSTP=DSTP, Tq=Tq: e.tensor_add(
                DSTP[:, hp * Tq + c0:hp * Tq + c0 + w], TT[:, 0:w], TU[:, 0:w]))
            wi_ += 1
        wfr.bump(1)
        p.q("tensor", lambda e: e.nop(), inc=(wfr, 1))
    p.barrier()
    vl = [p.sem(), p.sem()]
    vfr = p.sem()
    vds = [p.sem(), p.sem()]
    vfree = [p.sem(), p.sem()]
    vti = 0
    for vg in range(2):
        b = vg % 2
        p.dma("gpsimd", wvt[b][:].rearrange("p (k c) -> p k c", k=NCH),
              Wqkv[:, 2048 + vg * 512:2048 + (vg + 1) * 512].rearrange("(k p) c -> p k c", p=128),
              vl[b])
        for tt in range(20):
            pb = vti % 2
            psv = [p.PSA[:, 0:512], p.PSA[:, 512:1024]][pb]
            w0 = [(vfree[pb], vfree[pb].n)] if vti >= 2 else []
            if tt == 0:
                w0.append((vl[b], vl[b].n))
            for k in range(NCH):
                sh = -1 if k >= 4 else 0
                p.q("tensor",
                    lambda e, k=k, tt=tt, b=b, sh=sh, psv=psv: e.matmul(
                        psv[0:128, 0:512], XB[:, k * TE_X + tt * 128 + sh:k * TE_X + (tt + 1) * 128 + sh],
                        wvt[b][:, k * 512:(k + 1) * 512],
                        start=(k == 0), stop=(k == NCH - 1)),
                    wait=tuple(w0) if k == 0 else (),
                    inc=(vds[pb], 1) if k == NCH - 1 else None)
            vds[pb].bump(1)
            vfree[pb].bump(1)
            p.q("vector",
                lambda e, tt=tt, vg=vg, pb=pb, psv=psv: e.tensor_copy(
                    VT[:, tt * 1024 + vg * 512:tt * 1024 + (vg + 1) * 512], psv[0:128, 0:512]),
                wait=((vds[pb], vds[pb].n),), inc=(vfree[pb], 1))
            vti += 1
    p.barrier()
    at = p.sem()
    wol = p.sem()
    if debug:
        o_pt = p.outp("dbg_pt", (128, 512), F32)
        o_ss = p.outp("dbg_ss", (128, 512), F32)
        o_aot = p.outp("dbg_aot", (128, 1024), F32)
        dbs = p.sem()

    def astep(eng, fn, extra=()):
        v = at.bump(1)
        p.q(eng, fn, wait=tuple(extra))
        p.q(eng, lambda e: e.drain(), inc=(at, 1))
        for o in ("vector", "scalar", "tensor"):
            if o != eng:
                p.q(o, None, wait=((at, v),))

    for w_ in range(9):
        for qs2 in range(2):
            qcol = w_ * 256 + qs2 * 128
            for hp in range(8):
                for hh in range(2):
                    h = hp * 2 + hh
                    hb = hh * 64
                    astep("tensor", lambda e, hp=hp, hb=hb, qcol=qcol, w_=w_: e.matmul(
                        p.PSA[:, 0:512], QP[hb:hb + 64, hp * TE_Q + qcol:hp * TE_Q + qcol + 128],
                        KP[hb:hb + 64, hp * TE_K + w_ * 256:hp * TE_K + w_ * 256 + 512],
                        start=True, stop=True))
                    slab = (w_ * 2 + qs2) if w_ < 2 else (4 + qs2)
                    astep("vector", lambda e, slab=slab: e.tensor_add(
                        p.PSA[:, 0:512], p.PSA[:, 0:512], MK[:, slab * 512:(slab + 1) * 512]))
                    astep("vector", lambda e: e.reduce_max(nmc[:], p.PSA[:, 0:512], axis=AX.X, negate=True))
                    astep("vector", lambda e: e.tensor_scalar(nmc[:], nmc[:], 8.0, None, ALU.mult, ALU.bypass))
                    astep("scalar", lambda e: e.activation(out=PT[:], in_=p.PSA[:, 0:512], func=AF.Exp,
                                                           scale=8.0, bias=nmc[:], accum_out=den[:]))
                    astep("vector", lambda e: e.tensor_scalar(den[:], den[:], 1e-30, None,
                                                              ALU.add, ALU.bypass))
                    astep("scalar", lambda e: e.activation(out=den[:], in_=den[:], func=AF.Ln))
                    astep("scalar", lambda e: e.activation(out=den[:], in_=den[:], func=AF.Exp,
                                                           scale=-1.0))
                    astep("vector", lambda e: e.tensor_single_scalar(PT[:], PT[:], den[:], ALU.mult))
                    if debug and w_ == 2 and qs2 == 0 and hp == 0 and hh == 0:
                        astep("vector", lambda e: e.tensor_copy(SS2dbg[:], PT[:]))
                        p.dma("sync", o_pt, SS2dbg[:], dbs, wait=((at, at.n),))
                    for i in range(4):
                        astep("tensor", lambda e, i=i: e.transpose(
                            p.PST[:, i * 128:(i + 1) * 128], PT[:, i * 128:(i + 1) * 128], ident[:]))
                    astep("vector", lambda e: e.tensor_copy(PTT[:], p.PST[:, 0:512]))
                    for i in range(4):
                        kt = w_ * 2 + i
                        astep("tensor", lambda e, i=i, kt=kt, h=h, hb=hb: e.matmul(
                            p.PSB[hb:hb + 64, 0:128],
                            VT[:, kt * 1024 + h * 64:kt * 1024 + h * 64 + 64],
                            PTT[:, i * 128:(i + 1) * 128],
                            start=(i == 0), stop=(i == 3)))
                astep("vector", lambda e, hp=hp: e.tensor_copy(AOT[:, hp * 128:(hp + 1) * 128],
                                                               p.PSB[:, 0:128]))
            if debug and w_ == 2 and qs2 == 0:
                astep("vector", lambda e: e.tensor_copy(SS3dbg[:], AOT[:, 0:1024]))
                p.dma("sync", o_aot, SS3dbg[:], dbs, wait=((at, at.n),))
            for doh in range(4):
                wv = p.dma("gpsimd", wo_sb[:].rearrange("p (k d) -> p k d", k=NCH),
                           Wo.rearrange("(k p) n -> p k n", p=128)[:, :, doh * 256:(doh + 1) * 256],
                           wol, wait=((at, at.n),))
                p.q("tensor", None, wait=((wol, wv),))
                for do2 in range(2):
                    do = doh * 2 + do2
                    for hp in range(8):
                        last = (hp == 7)
                        astep("tensor", lambda e, do2=do2, hp=hp, last=last: e.matmul(
                            p.PSB[:, 512 + (do2 % 2) * 128:512 + (do2 % 2) * 128 + 128],
                            wo_sb[:, hp * 256 + do2 * 128:hp * 256 + do2 * 128 + 128],
                            AOT[:, hp * 128:(hp + 1) * 128],
                            start=(hp == 0), stop=last))
                    gcol = qcol + 384
                    xbcol = qcol + 256
                    if gcol + 128 > XP0:
                        off = max(0, XP0 - gcol)
                        astep("vector", lambda e, do=do, do2=do2, gcol=gcol, xbcol=xbcol, off=off: e.tensor_add(
                            XPB[:, do * TE_P + gcol + off - XP0:do * TE_P + gcol - XP0 + 128],
                            p.PSB[:, 512 + (do2 % 2) * 128 + off:512 + (do2 % 2) * 128 + 128],
                            XB[:, do * TE_X + xbcol + off:do * TE_X + xbcol + 128]))
    p.barrier()

    XN_f = VT
    W1b = AOT
    mu = XB[0:1, 0:4096].bitcast(F32)
    e2 = XB[0:1, 4096:8192].bitcast(F32)
    r_ = XB[0:1, 8192:12288].bitcast(F32)
    mr = XB[0:1, 12288:16384].bitcast(F32)
    rbc = XB[:, 16384:18432]
    mrbc = XB[:, 18432:20480]
    rb_ = XB[0:1, 10240:12288]
    mrb_ = XB[0:1, 14336:16384]
    sqb2 = PT
    TF = 2048

    def xs(c, c0, w):
        sh = -1 if c >= 4 else 0
        return XPB[:, c * TE_P + 64 + c0 + sh:c * TE_P + 64 + c0 + sh + w]

    psx = p.PSA[0:1, 0:512]
    psq = p.PSA[0:1, 512:1024]
    evs2 = p.sem()
    tds2 = p.sem()
    sqs2 = p.sem()
    for gi_, (c0, w) in enumerate(chunks_of(TF, 512)):
        for c in range(NCH):
            sv = sqs2.bump(1)
            p.q("scalar",
                lambda e, c=c, c0=c0, w=w: e.activation(out=sqb2[:, 0:w], in_=xs(c, c0, w),
                                                        func=AF.Square),
                wait=((tds2, tds2.n),) if sv > 1 else (), inc=(sqs2, 1))
            p.q("tensor",
                lambda e, c=c, c0=c0, w=w: e.matmul(psx[0:1, 0:w], ones[:, 0:1], xs(c, c0, w),
                                                    start=(c == 0), stop=(c == NCH - 1)),
                wait=((evs2, evs2.n),) if (c == 0 and gi_ > 0) else ())
            tds2.bump(1)
            p.q("tensor",
                lambda e, w=w, c=c: e.matmul(psq[0:1, 0:w], ones[:, 0:1], sqb2[:, 0:w],
                                             start=(c == 0), stop=(c == NCH - 1)),
                wait=((sqs2, sv),) + (((evs2, evs2.n),) if (c == 0 and gi_ > 0) else ()),
                inc=(tds2, 1))
        dv = tds2.n
        ev = evs2.bump(1)
        p.q("vector", lambda e, c0=c0, w=w: e.tensor_copy(mu[:, c0:c0 + w], psx[0:1, 0:w]),
            wait=((tds2, dv),))
        p.q("vector", lambda e, c0=c0, w=w: e.tensor_copy(e2[:, c0:c0 + w], psq[0:1, 0:w]),
            inc=(evs2, 1))
    p.barrier()
    p.q("vector", lambda e: e.tensor_scalar(mu[:, 0:TF], mu[:, 0:TF], 1.0 / D, None, ALU.mult, ALU.bypass))
    p.q("vector", lambda e: e.tensor_scalar(e2[:, 0:TF], e2[:, 0:TF], 1.0 / D, None, ALU.mult, ALU.bypass))
    p.q("vector", lambda e: e.tensor_mul(r_[:, 0:TF], mu[:, 0:TF], mu[:, 0:TF]))
    p.q("vector", lambda e: e.tensor_sub(e2[:, 0:TF], e2[:, 0:TF], r_[:, 0:TF]))
    p.q("vector", lambda e: e.tensor_scalar(e2[:, 0:TF], e2[:, 0:TF], 1e-5, None, ALU.add, ALU.bypass))
    p.barrier()
    p.q("scalar", lambda e: e.activation(out=r_[:, 0:TF], in_=e2[:, 0:TF], func=AF.Ln))
    p.barrier()
    p.q("scalar", lambda e: e.activation(out=r_[:, 0:TF], in_=r_[:, 0:TF], func=AF.Exp, scale=-0.5))
    p.barrier()
    p.q("vector", lambda e: e.tensor_mul(mr[:, 0:TF], mu[:, 0:TF], r_[:, 0:TF]))
    p.q("vector", lambda e: e.tensor_copy(rb_[:, 0:TF], r_[:, 0:TF]))
    p.q("vector", lambda e: e.tensor_copy(mrb_[:, 0:TF], mr[:, 0:TF]))
    p.barrier()
    bmm2 = p.sem()
    bcp2 = p.sem()
    for (src, dst) in ((rb_, rbc), (mrb_, mrbc)):
        for (c0, w) in chunks_of(TF, 512):
            mv = bmm2.bump(1)
            p.q("tensor", lambda e, src=src, c0=c0, w=w: e.matmul(
                p.PSC[:, 0:w], ones[0:1, :], src[:, c0:c0 + w], start=True, stop=True),
                wait=((bcp2, bcp2.n),) if bcp2.n > 0 else (), inc=(bmm2, 1))
            bcp2.bump(1)
            p.q("vector", lambda e, dst=dst, c0=c0, w=w: e.tensor_copy(
                dst[:, c0:c0 + w], p.PSC[:, 0:w]), wait=((bmm2, mv),), inc=(bcp2, 1))
    p.barrier()
    for c in range(NCH):
        for (c0, w) in chunks_of(TF, 512):
            p.q("vector", lambda e, c=c, c0=c0, w=w: e.tensor_mul(
                PTT[:, 0:w], xs(c, c0, w), rbc[:, c0:c0 + w]))
            p.q("vector", lambda e, c0=c0, w=w: e.tensor_sub(PTT[:, 0:w], PTT[:, 0:w],
                                                             mrbc[:, c0:c0 + w]))
            p.q("vector", lambda e, c=c, c0=c0, w=w: e.tensor_scalar(
                XN_f[:, c * TF + c0:c * TF + c0 + w], PTT[:, 0:w],
                lngb[:, c:c + 1], lnbb[:, c:c + 1], ALU.mult, ALU.add))
    p.barrier()

    AT = QP[:, 0:22 * 512]
    IT = KP[:, 0:22 * 512]
    w1l = p.sem()
    w1f = p.sem()
    w2l = [p.sem(), p.sem()]
    w2f = p.sem()
    fds = [p.sem(), p.sem()]
    ffr = [p.sem(), p.sem()]
    gmu2 = p.sem()
    psF = [p.PSB[:, 0:512], p.PSB[:, 512:1024]]
    psY = [p.PSA[:, i * 512:(i + 1) * 512] for i in range(4)]
    yds = p.sem()
    yev = p.sem()
    stx = [p.sem(), p.sem()]
    fi = 0
    sti = 0
    w2i = 0
    for gi_, (c0, w) in enumerate(chunks_of(TF, 512)):
        for mj in range(44):
            isa_ = (mj < 22)
            mc = mj if isa_ else mj - 22
            wcol0 = mc * 128 if isa_ else 2730 + mc * 128
            mrows = 128 if (mc < 21) else 42
            lv = p.dma("gpsimd", W1b[:, 0:NCH * mrows].rearrange("p (k c) -> p k c", k=NCH),
                       W1[:, wcol0:wcol0 + mrows].rearrange("(k p) c -> p k c", p=128),
                       w1l, wait=((w1f, w1f.n),) if fi >= 1 else ())
            pb = fi % 2
            w0 = [(ffr[pb], ffr[pb].n)] if fi >= 2 else []
            w0.append((w1l, lv))
            for k in range(NCH):
                p.q("tensor",
                    lambda e, k=k, c0=c0, w=w, mrows=mrows, pb=pb: e.matmul(
                        psF[pb][0:mrows, 0:w], W1b[:, k * mrows:(k + 1) * mrows],
                        XN_f[:, k * TF + c0:k * TF + c0 + w],
                        start=(k == 0), stop=(k == NCH - 1)),
                    wait=tuple(w0) if k == 0 else (),
                    inc=(fds[pb], 1) if k == NCH - 1 else None)
            fds[pb].bump(1)
            w1f.bump(1)
            p.q("tensor", lambda e: e.nop(), inc=(w1f, 1))
            if isa_:
                ffr[pb].bump(1)
                p.q("vector",
                    lambda e, mc=mc, w=w, mrows=mrows, pb=pb: e.tensor_scalar(
                        AT[0:mrows, mc * 512:mc * 512 + w], psF[pb][0:mrows, 0:w],
                        b1b[0:mrows, mc:mc + 1], None, ALU.add, ALU.bypass),
                    wait=((fds[pb], fds[pb].n),), inc=(ffr[pb], 1))
            else:
                gv = gmu2.bump(1)
                p.q("scalar",
                    lambda e, mc=mc, w=w, mrows=mrows, pb=pb: e.activation(
                        out=PTT[0:mrows, 0:w], in_=psF[pb][0:mrows, 0:w], func=AF.Gelu,
                        bias=b1b[0:mrows, 22 + mc:22 + mc + 1]),
                    wait=((fds[pb], fds[pb].n),) + (((gmu2, gv - 1),) if gv >= 2 else ()),
                    inc=(gmu2, 1))
                ffr[pb].bump(1)
                p.q("vector",
                    lambda e, mc=mc, w=w, mrows=mrows, pb=pb: e.tensor_mul(
                        IT[0:mrows, mc * 512:mc * 512 + w], PTT[0:mrows, 0:w],
                        AT[0:mrows, mc * 512:mc * 512 + w]),
                    wait=((gmu2, gmu2.n),), inc=(ffr[pb], 1))
                gmu2.bump(1)
                p.q("vector", lambda e: e.drain(), inc=(gmu2, 1))
            fi += 1
        for sw in range(2):
            for mk in range(22):
                b2_ = w2i % 2
                mrows = 128 if (mk < 21) else 42
                lv = p.dma("gpsimd", w2t[b2_][0:mrows, :],
                           W2[mk * 128:mk * 128 + mrows, sw * 512:(sw + 1) * 512],
                           w2l[b2_], wait=((w2f, w2f.n - 1),) if w2i >= 2 else ())
                for do2 in range(4):
                    if mk == 0 and do2 == 0:
                        wv = ((w2l[b2_], lv), (ffr[0], ffr[0].n), (ffr[1], ffr[1].n),
                              (yev, yev.n))
                    elif do2 == 0:
                        wv = ((w2l[b2_], lv),)
                    else:
                        wv = ()
                    p.q("tensor",
                        lambda e, mk=mk, do2=do2, mrows=mrows, b2_=b2_: e.matmul(
                            psY[do2][0:128, 0:512],
                            w2t[b2_][0:mrows, do2 * 128:(do2 + 1) * 128],
                            IT[0:mrows, mk * 512:(mk + 1) * 512],
                            start=(mk == 0), stop=(mk == 21)),
                        wait=wv,
                        inc=(yds, 1) if (mk == 21 and do2 == 3) else None)
                w2f.bump(1)
                p.q("tensor", lambda e: e.nop(), inc=(w2f, 1))
                w2i += 1
            yds.bump(1)
            for do2 in range(4):
                do = sw * 4 + do2
                bs = sti % 2
                wv = [(yds, yds.n)]
                if sti >= 2:
                    wv.append((stx[bs], 16 * (sti // 2)))
                p.q("vector",
                    lambda e, do2=do2, do=do, w=w, bs=bs: e.tensor_scalar(
                        xstg[bs][:, 0:w], psY[do2][0:128, 0:w], b2b[:, do:do + 1], None,
                        ALU.add, ALU.bypass),
                    wait=tuple(wv))
                p.q("vector",
                    lambda e, do=do, c0=c0, w=w, bs=bs: e.tensor_add(
                        xstg[bs][:, 0:w], xstg[bs][:, 0:w],
                        XPB[:, do * TE_P + 64 + c0:do * TE_P + 64 + c0 + w]))
                ev = yev.bump(1)
                p.q("vector", lambda e: e.drain(), inc=(yev, 1))
                p.dma("sync", o_x[do * 128:(do + 1) * 128, c0:c0 + w], xstg[bs][:, 0:w],
                      stx[bs], wait=((yev, ev),))
                sti += 1
    finish(p, [stx[0], stx[1]])
    return p


def l2f_tables(s0):
    DH, WIN = 64, 256
    half = np.arange(0, DH, 2, dtype=np.float64)
    inv_freq = 1.0 / 10000.0 ** (half / DH)
    base_scale = (half + 0.4 * DH) / (1.4 * DH)

    def tabs(pos):
        t = pos.astype(np.float64)
        fr = t[:, None] * inv_freq[None, :]
        fr = np.concatenate([fr, fr], -1)
        power = (t - 2048) / (WIN // 2)
        sc = base_scale[None, :] ** power[:, None]
        sc = np.concatenate([sc, sc], -1)
        cosq = (np.cos(fr) * sc).T
        sinq = (np.sin(fr) * sc).T
        cosk = (np.cos(fr) / sc).T
        sink = (np.sin(fr) / sc).T
        st = lambda a: np.ascontiguousarray(np.concatenate([a, a], 0), np.float32)
        return st(cosq), st(sinq), st(cosk), st(sink)

    qpos = np.arange(s0 - 256, s0 + 2048)
    kpos = np.arange(s0 - 512, s0 + 2048)
    CQ, SQ, _, _ = tabs(qpos)
    _, _, CK, SK = tabs(kpos)
    return CQ, SQ, CK, SK


def l2f_masks(j):
    o = np.arange(256)[:, None]
    jj = np.arange(512)[None, :]
    normal = np.where(((jj < 256) & (jj >= o)) | ((jj >= 256) & (jj - 256 <= o)),
                      0.0, -1e30).astype(np.float32)
    noprev = normal.copy()
    noprev[:, :256] = -1e30
    slabs = []
    for wsel in range(2):
        msk = normal if (j == 1) else noprev
        slabs.append(msk[0:128]); slabs.append(msk[128:256])
    slabs.append(normal[0:128]); slabs.append(normal[128:256])
    return np.ascontiguousarray(np.concatenate(slabs, axis=1), np.float32)


def l2f_inputs(xglob, b, j, z, li):
    s0 = j * 2048
    pad = np.zeros((512, 1024), np.float32)
    xs = np.concatenate([pad, xglob[b]], 0)[s0:s0 + 2560 + 0, :]
    m = {
        "x": np.ascontiguousarray(xs.T),
        "Wqkv": z['L2_ffn_Wqkv'][li],
        "Wo": z['L2_ffn_Wo'][li],
        "qsT": np.ascontiguousarray(np.concatenate([z['L2_ffn_q_scale'][li]] * 2).reshape(128, 1)),
        "ksT": np.ascontiguousarray(np.concatenate([z['L2_ffn_k_scale'][li]] * 2).reshape(128, 1)),
        "lngT": np.ascontiguousarray(z['L2_ffn_ln_g'][li].reshape(8, 128).T),
        "lnbT": np.ascontiguousarray(z['L2_ffn_ln_b'][li].reshape(8, 128).T),
        "W1": z['L2_ffn_W1'][li],
        "W2": z['L2_ffn_W2'][li],
        "b2T": np.ascontiguousarray(z['L2_ffn_b2'][li].reshape(8, 128).T),
        "MKI": l2f_masks(j),
    }
    b1 = z['L2_ffn_b1'][li]
    b1pk = np.zeros((128, 44), np.float32)
    for ci in range(22):
        seg = b1[ci * 128:min((ci + 1) * 128, 2730)]
        b1pk[:len(seg), ci] = seg
        seg = b1[2730 + ci * 128:2730 + min((ci + 1) * 128, 2730)]
        b1pk[:len(seg), 22 + ci] = seg
    m["b1pk"] = b1pk
    CQ, SQ, CK, SK = l2f_tables(s0)
    m["CQ"], m["SQ"], m["CK"], m["SK"] = CQ, SQ, CK, SK
    return m


def build_l3a():
    p = Prog("l3a")
    x = p.inp("x", (D, T), F32)
    wqT = p.inp("wqT", (D, D), F32)
    wkT = p.inp("wkT", (D, D), F32)
    wvT = p.inp("wvT", (D, D), F32)
    RC = p.inp("RC", (128, T), F32)
    RS = p.inp("RS", (128, T), F32)
    W1 = p.inp("W1", (8, D, 4096), F32)
    b1T = p.inp("b1T", (8, 128, 32), F32)
    W2 = p.inp("W2", (8, 4096, D), F32)
    b2r = p.inp("b2r", (8, D), F32)
    Wr = p.inp("Wr", (D, 8), F32)
    brr = p.inp("brr", (1, 8), F32)
    o_q = p.outp("qf", (D, T), BF16)
    o_k = p.outp("kf", (D, T), BF16)
    o_v = p.outp("vt", (T, D), BF16)
    o_g = p.outp("gffn", (D, T), BF16)

    ones = p.sb("ones", (128, 128), BF16)
    ident = p.sb("ident", (128, 128), BF16)
    pm = np.zeros((128, 128), np.float32)
    for blk in range(2):
        for kk in range(32):
            pm[blk * 64 + 2 * kk + 1, blk * 64 + 2 * kk] = -1.0
            pm[blk * 64 + 2 * kk, blk * 64 + 2 * kk + 1] = 1.0
    pswap = p.sb("pswap", (128, 128), BF16)
    ld0 = p.sem()
    p.dma("gpsimd", ones[:], p.const(np.ones((128, 128), BF), "cones"), ld0)
    p.dma("gpsimd", ident[:], p.const(np.eye(128, dtype=BF), "cident"), ld0)
    p.dma("gpsimd", pswap[:], p.const(pm.astype(BF), "cpswap3"), ld0)

    XN = p.sb("XN", (128, NCH * T), BF16)
    scrA = p.sb("scrA", (128, NCH * T), BF16)
    scrB = p.sb("scrB", (128, NCH * T), BF16)

    wq_sb = p.sb("wq_sb", (128, NCH * D), BF16)
    wk_sb = p.sb("wk_sb", (128, NCH * D), BF16)
    wv_sb = p.sb("wv_sb", (128, NCH * D), BF16)
    RCb = scrB[:, 2048:4096]
    RSb = scrB[:, 4096:6144]
    wr_sb = p.sb("wr_sb", (128, NCH * 8), BF16)
    brb = p.sb("brb", (1, 8), BF16)
    logits = p.sb("logits", (128, 16 * 8), F32)
    wcomb = p.sb("wcomb", (128, 16 * 8), F32)
    m1c = p.sb("m1c", (128, 1), F32)
    w1b = [p.sb(f"w1b{i}", (128, NCH * 128), BF16) for i in range(2)]
    w2b = [p.sb("w2b0", (128, 4096), BF16)]
    b1c = p.sb("b1c", (128, 32), F32)
    relb = p.sb("relb", (128, 32), BF16)
    r2stg = p.sb("r2stg", (1, 512), BF16)
    ctrow = p.sb("ctrow", (1, D), F32)
    ctrowb = p.sb("ctrowb", (1, D), BF16)
    ctr = p.sb("ctr", (128, 8), F32)
    r2bc = p.sb("r2bc", (128, 512), BF16)
    qstg = [p.sb(f"qstg{i}", (128, 512), BF16) for i in range(2)]
    gstg = [p.sb(f"gstg{i}", (128, 512), BF16) for i in range(2)]

    p.dma("gpsimd", wq_sb[:].rearrange("p (k n) -> p k n", k=NCH),
          wqT.rearrange("(k p) n -> p k n", p=128), ld0)
    p.dma("gpsimd", wk_sb[:].rearrange("p (k n) -> p k n", k=NCH),
          wkT.rearrange("(k p) n -> p k n", p=128), ld0)
    p.dma("gpsimd", wv_sb[:].rearrange("p (k n) -> p k n", k=NCH),
          wvT.rearrange("(k p) n -> p k n", p=128), ld0)
    p.dma("gpsimd", wr_sb[:].rearrange("p (k n) -> p k n", k=NCH),
          Wr.rearrange("(k p) n -> p k n", p=128), ld0)
    p.dma("gpsimd", RCb[:], RC, ld0)
    p.dma("gpsimd", RSb[:], RS, ld0)
    p.dma("gpsimd", brb[:], brr, ld0)
    for e in ENGINES:
        p.q(e, None, wait=((ld0, ld0.n),))

    scr = ln_scratch(p, scrA, scrB, "f")
    emit_ln(p, x, XN, ones[:, 0:1], ones[0:1, :], scr)
    xn = XN

    rs = p.sem()
    stq = [p.sem(), p.sem()]
    qi = 0
    QSC = scrA[:, 0:512]
    ROT = scrA[:, 512:1024]
    for (wsb, outp) in ((wq_sb, o_q), (wk_sb, o_k)):
        for m in range(NCH):
            for ci, (c0, w) in enumerate(chunks_of(T, 512)):
                pb = qi % 2
                ps = [p.PSA[:, 0:512], p.PSA[:, 512:1024]][pb]

                def qs_(eng, fn, extra=()):
                    v = rs.bump(1)
                    p.q(eng, fn, wait=tuple(extra))
                    p.q(eng, lambda e: e.drain(), inc=(rs, 1))
                    for o in ("vector", "scalar", "tensor"):
                        if o != eng:
                            p.q(o, None, wait=((rs, v),))

                for k in range(NCH):
                    p.q("tensor",
                        lambda e, k=k, m=m, c0=c0, w=w, wsb=wsb, ps=ps: e.matmul(
                            ps[0:128, 0:w], wsb[:, k * D + m * 128:k * D + (m + 1) * 128],
                            xn[:, k * T + c0:k * T + c0 + w], start=(k == 0), stop=(k == NCH - 1)),
                        wait=() if k > 0 else ((rs, rs.n),))
                qs_("tensor", lambda e: e.nop())
                qs_("vector", lambda e, w=w, ps=ps: e.tensor_copy(QSC[:, 0:w], ps[0:128, 0:w]))
                qs_("tensor", lambda e, w=w: e.matmul(p.PSB[:, 0:w], pswap[:], QSC[:, 0:w],
                                                      start=True, stop=True))
                qs_("vector", lambda e, w=w, c0=c0: e.tensor_mul(QSC[:, 0:w], QSC[:, 0:w],
                                                                 RCb[:, c0:c0 + w]))
                qs_("vector", lambda e, w=w, c0=c0: e.tensor_mul(ROT[:, 0:w], p.PSB[:, 0:w],
                                                                 RSb[:, c0:c0 + w]))
                bs = qi % 2
                wv = ((stq[bs], 16 * (qi // 2)),) if qi >= 2 else ()
                qs_("vector", lambda e, w=w, bs=bs: e.tensor_add(qstg[bs][:, 0:w], QSC[:, 0:w],
                                                                 ROT[:, 0:w]), extra=wv)
                v2 = rs.n
                p.dma("sync", outp[m * 128:(m + 1) * 128, c0:c0 + w], qstg[bs][:, 0:w],
                      stq[bs], wait=((rs, v2),))
                qi += 1
    p.barrier()
    vds = [p.sem(), p.sem()]
    vfr = [p.sem(), p.sem()]
    stv = [p.sem(), p.sem()]
    vstg = [scrB[:, 0:1024], scrB[:, 1024:2048]]
    vi = 0
    for tt in range(16):
        for vg in range(2):
            pb = vi % 2
            ps = [p.PSA[:, 0:512], p.PSA[:, 512:1024]][pb]
            w0 = [(vfr[pb], vfr[pb].n)] if vi >= 2 else []
            for k in range(NCH):
                p.q("tensor",
                    lambda e, k=k, tt=tt, vg=vg, ps=ps: e.matmul(
                        ps[0:128, 0:512], xn[:, k * T + tt * 128:k * T + (tt + 1) * 128],
                        wv_sb[:, k * D + vg * 512:k * D + (vg + 1) * 512],
                        start=(k == 0), stop=(k == NCH - 1)),
                    wait=tuple(w0) if k == 0 else (),
                    inc=(vds[pb], 1) if k == NCH - 1 else None)
            vds[pb].bump(1)
            vfr[pb].bump(1)
            bs = tt % 2
            wv = ((vds[pb], vds[pb].n),)
            if tt >= 2 and vg == 0:
                wv = wv + ((stv[bs], 16 * (tt // 2)),)
            p.q("vector",
                lambda e, vg=vg, bs=bs, ps=ps: e.tensor_copy(
                    vstg[bs][:, vg * 512:(vg + 1) * 512], ps[0:128, 0:512]),
                wait=wv, inc=(vfr[pb], 1))
            vi += 1
        vv = p.sem()
        vv.bump(1)
        p.q("vector", lambda e: e.drain(), inc=(vv, 1))
        p.dma("sync", o_v[tt * 128:(tt + 1) * 128, :], vstg[tt % 2][:], stv[tt % 2],
              wait=((vv, 1),))
    p.barrier()

    rds = p.sem()
    rvs = p.sem()
    for t in range(16):
        for k in range(NCH):
            p.q("tensor",
                lambda e, k=k, t=t: e.matmul(p.PSC[:, 0:8],
                                             xn[:, k * T + t * 128:k * T + (t + 1) * 128],
                                             wr_sb[:, k * 8:(k + 1) * 8],
                                             start=(k == 0), stop=False),
                wait=((rvs, rvs.n),) if (k == 0 and t > 0) else ())
        rds.bump(1)
        p.q("tensor", lambda e, t=t: e.matmul(p.PSC[:, 0:8], ones[0:1, :], brb[:],
                                              start=False, stop=True), inc=(rds, 1))
        rvs.bump(1)
        p.q("vector", lambda e, t=t: e.tensor_copy(logits[:, t * 8:(t + 1) * 8], p.PSC[:, 0:8]),
            wait=((rds, rds.n),), inc=(rvs, 1))
    p.barrier()
    hz = p.sem()
    for t in range(16):
        lt = logits[:, t * 8:(t + 1) * 8]
        wt = wcomb[:, t * 8:(t + 1) * 8]
        hv = hz.bump(1)
        p.q("vector", lambda e, lt=lt: e.reduce_max(m1c[:], lt, axis=AX.X))
        p.q("vector", lambda e: e.drain(), inc=(hz, 1))
        p.q("vector", None, wait=((hz, hv),))
        p.q("vector", lambda e, lt=lt, wt=wt: e.tensor_single_scalar(wt, lt, m1c[:], ALU.is_ge))
        p.q("vector", lambda e: e.drain(), inc=(hz, 1))
        hz.bump(1)
    p.barrier()

    YC = scrB
    YE = p.sb("YE", (128, 16 * 1024), BF16)
    p.q("vector", lambda e: e.memset(ctrow[:], 0.0))
    p.q("vector", lambda e: e.memset(YC[:], 0.0))
    p.barrier()
    el = [p.sem(), p.sem()]
    e2l = [p.sem(), p.sem()]
    ef = p.sem()
    w1u = p.sem()
    hds = [p.sem(), p.sem()]
    hfr = [p.sem(), p.sem()]
    yds = [p.sem(), p.sem()]
    yfr = [p.sem(), p.sem()]
    rst = p.sem()
    cbs = p.sem()
    psH = [p.PSB[:, 0:512], p.PSB[:, 512:1024]]
    psYl = [p.PSA[:, 0:512], p.PSA[:, 512:1024]]
    psR = [p.PSA[0:1, 1024:1536], p.PSA[0:1, 1536:2048]]
    H = scrA
    hi2 = 0
    yi2 = 0
    for e_ in range(8):
        eb = e_ % 2
        lb = p.dma("gpsimd", b1c[:], b1T[e_], e2l[1],
                   wait=((ef, ef.n),) if e_ >= 2 else ())
        rv = rst.bump(1)
        p.q("vector", lambda e: e.tensor_single_scalar(relb[:], b1c[:], 0.0, ALU.max),
            wait=((e2l[1], lb),))
        p.q("vector", lambda e: e.drain(), inc=(rst, 1))
        p.q("tensor", None, wait=((rst, rv),))
        p.q("vector", lambda e: e.memset(YE[:], 0.0))
        p.q("vector", lambda e: e.drain(), inc=(rst, 1))
        rst.bump(1)
        p.q("tensor", None, wait=((rst, rst.n),))
        for g8 in range(4):
            for m8 in range(8):
                mh = g8 * 8 + m8
                wb_ = hi2 // 4 % 2
                lv = p.dma("gpsimd", w1b[wb_][:].rearrange("p (k c) -> p k c", k=NCH),
                           W1[e_][:, mh * 128:(mh + 1) * 128].rearrange("(k p) c -> p k c", p=128),
                           el[wb_], wait=((w1u, max(0, w1u.n - 1)),) if w1u.n >= 1 else ())
                for ci, (c0, w) in enumerate(chunks_of(T, 512)):
                    pb = hi2 % 2
                    w0 = [(hfr[pb], hfr[pb].n)] if hi2 >= 2 else []
                    if ci == 0:
                        w0.append((el[wb_], lv))
                    for k in range(NCH):
                        p.q("tensor",
                            lambda e, k=k, c0=c0, w=w, wb_=wb_, pb=pb: e.matmul(
                                psH[pb][0:128, 0:w], w1b[wb_][:, k * 128:(k + 1) * 128],
                                xn[:, k * T + c0:k * T + c0 + w], start=(k == 0), stop=(k == NCH - 1)),
                            wait=tuple(w0) if k == 0 else (),
                            inc=(hds[pb], 1) if k == NCH - 1 else None)
                    hds[pb].bump(1)
                    hfr[pb].bump(1)
                    p.q("scalar",
                        lambda e, m8=m8, mh=mh, c0=c0, w=w, pb=pb: e.activation(
                            out=H[:, m8 * T + c0:m8 * T + c0 + w],
                            in_=psH[pb][0:128, 0:w], func=AF.Relu, bias=b1c[:, mh:mh + 1]),
                        wait=((hds[pb], hds[pb].n),), inc=(hfr[pb], 1))
                    hi2 += 1
                w1u.bump(1)
                p.q("tensor", lambda e: e.nop(), inc=(w1u, 1))
            for dg in range(2):
                yb = 0
                lw = p.dma("gpsimd", w2b[yb][:].rearrange("p (k c) -> p k c", k=8),
                           W2[e_][g8 * 1024:(g8 + 1) * 1024, dg * 512:(dg + 1) * 512]
                           .rearrange("(k p) c -> p k c", p=128),
                           e2l[0], wait=((yfr[0], yfr[0].n), (yfr[1], yfr[1].n)) if yi2 >= 1 else ())
                for k8 in range(8):
                    p.q("tensor",
                        lambda e, k8=k8, g8=g8, dg=dg, yb=yb: e.matmul(
                            psR[dg], relb[:, g8 * 8 + k8:g8 * 8 + k8 + 1],
                            w2b[yb][:, k8 * 512:(k8 + 1) * 512],
                            start=(g8 == 0 and k8 == 0), stop=(g8 == 3 and k8 == 7)),
                        wait=((e2l[0], e2l[0].n), (hfr[0], hfr[0].n), (hfr[1], hfr[1].n),
                              (rst, rst.n)) if k8 == 0 else ())
                for tt in range(16):
                    pb = yi2 % 2
                    w0 = [(yfr[pb], yfr[pb].n)] if yi2 >= 2 else []
                    for k8 in range(8):
                        p.q("tensor",
                            lambda e, k8=k8, tt=tt, pb=pb, yb=yb: e.matmul(
                                psYl[pb][0:128, 0:512],
                                H[:, k8 * T + tt * 128:k8 * T + (tt + 1) * 128],
                                w2b[yb][:, k8 * 512:(k8 + 1) * 512],
                                start=(k8 == 0), stop=(k8 == 7)),
                            wait=tuple(w0) if k8 == 0 else (),
                            inc=(yds[pb], 1) if k8 == 7 else None)
                    yds[pb].bump(1)
                    yfr[pb].bump(1)
                    p.q("vector",
                        lambda e, tt=tt, dg=dg, pb=pb: e.tensor_add(
                            YE[:, tt * 1024 + dg * 512:tt * 1024 + (dg + 1) * 512],
                            YE[:, tt * 1024 + dg * 512:tt * 1024 + (dg + 1) * 512],
                            psYl[pb][0:128, 0:512]),
                        wait=((yds[pb], yds[pb].n),), inc=(yfr[pb], 1))
                    yi2 += 1
        cv = cbs.bump(1)
        p.q("vector", lambda e: e.tensor_add(ctrow[:, 0:512], ctrow[:, 0:512], psR[0]),
            wait=((yfr[0], yfr[0].n), (yfr[1], yfr[1].n)))
        p.q("vector", lambda e: e.tensor_add(ctrow[:, 512:1024], ctrow[:, 512:1024], psR[1]))
        p.q("vector", lambda e: e.drain(), inc=(cbs, 1))
        for dg in range(2):
            bv0 = cbs.bump(1)
            p.q("vector", lambda e, dg=dg: e.tensor_copy(r2stg[:], psR[dg]),
                wait=((cbs, cbs.n - 1),))
            p.q("vector", lambda e: e.drain(), inc=(cbs, 1))
            bv = cbs.bump(1)
            p.q("tensor", lambda e: e.matmul(
                p.PSC[:, 0:512], ones[0:1, :], r2stg[:], start=True, stop=True),
                wait=((cbs, bv0),), inc=(cbs, 1))
            bv2 = cbs.bump(1)
            p.q("vector", lambda e: e.tensor_copy(r2bc[:], p.PSC[:, 0:512]),
                wait=((cbs, bv),))
            p.q("vector", lambda e: e.drain(), inc=(cbs, 1))
            for tt in range(16):
                p.q("vector", lambda e, tt=tt, dg=dg: e.tensor_sub(
                    qstg[0][:, 0:512], YE[:, tt * 1024 + dg * 512:tt * 1024 + (dg + 1) * 512],
                    r2bc[:]))
                p.q("vector", lambda e, tt=tt, dg=dg, e_=e_: e.scalar_tensor_tensor(
                    YC[:, tt * 1024 + dg * 512:tt * 1024 + (dg + 1) * 512],
                    qstg[0][:, 0:512], wcomb[:, tt * 8 + e_:tt * 8 + e_ + 1],
                    YC[:, tt * 1024 + dg * 512:tt * 1024 + (dg + 1) * 512],
                    ALU.mult, ALU.add))
            cv2 = cbs.bump(1)
            p.q("vector", lambda e: e.drain(), inc=(cbs, 1))
        ef.bump(1)
        p.q("vector", lambda e: e.engine_nop(), inc=(ef, 1))
    p.barrier()
    b2flat = p.inp("b2flat", (1, 8 * D), F32)
    b2fb = YE[0:1, 0:16 * D].bitcast(F32)
    ldb = p.sem()
    p.dma("gpsimd", b2fb, b2flat, ldb)
    p.q("vector", None, wait=((ldb, 16),))
    for e_ in range(8):
        p.q("vector", lambda e, e_=e_: e.tensor_add(ctrow[:], ctrow[:], b2fb[:, e_ * D:(e_ + 1) * D]))
    p.q("vector", lambda e: e.tensor_copy(ctrowb[:], ctrow[:]))
    p.barrier()
    trc = p.sem()
    for c in range(NCH):
        tv = trc.bump(1)
        p.q("tensor", lambda e, c=c: e.transpose(p.PST[0:128, 0:1], ctrowb[:, c * 128:(c + 1) * 128],
                                                 ident[0:1, 0:1]),
            wait=((trc, trc.n - 1),) if trc.n >= 2 else (), inc=(trc, 1))
        trc.bump(1)
        p.q("vector", lambda e, c=c: e.tensor_copy(ctr[:, c:c + 1], p.PST[0:128, 0:1]),
            wait=((trc, tv),), inc=(trc, 1))
    p.barrier()
    YCT = scrA
    trs = p.sem()
    cpd = p.sem()
    for t in range(16):
        for dc in range(NCH):
            last = (dc == NCH - 1)
            if last:
                trs.bump(1)
            p.q("tensor",
                lambda e, t=t, dc=dc: e.transpose(
                    p.PST[:, dc * 128:(dc + 1) * 128],
                    YC[:, t * 1024 + dc * 128:t * 1024 + (dc + 1) * 128], ident[:]),
                wait=((cpd, t),) if (dc == 0 and t >= 1) else (),
                inc=(trs, 1) if last else None)
        cpd.bump(1)
        p.q("vector",
            lambda e, t=t: e.tensor_copy(
                YCT[:].rearrange("p (dc tt) -> p dc tt", dc=NCH)[:, :, t * 128:(t + 1) * 128],
                p.PST[:].rearrange("p (dc tt) -> p dc tt", dc=NCH)),
            wait=((trs, trs.n),), inc=(cpd, 1))
    p.barrier()
    stg = [p.sem(), p.sem()]
    evg = p.sem()
    gi2 = 0
    for c in range(NCH):
        for (c0, w) in chunks_of(T, 512):
            bs = gi2 % 2
            wv = ((stg[bs], 16 * (gi2 // 2)),) if gi2 >= 2 else ()
            ev = evg.bump(1)
            p.q("scalar",
                lambda e, c=c, c0=c0, w=w, bs=bs: e.activation(
                    out=gstg[bs][:, 0:w], in_=YCT[:, c * T + c0:c * T + c0 + w],
                    func=AF.Silu, bias=ctr[:, c:c + 1]),
                wait=wv)
            p.q("scalar", lambda e: e.drain(), inc=(evg, 1))
            p.dma("sync", o_g[c * 128:(c + 1) * 128, c0:c0 + w], gstg[bs][:, 0:w], stg[bs],
                  wait=((evg, ev),))
            gi2 += 1
    finish(p, [stq[0], stq[1], stv[0], stv[1], stg[0], stg[1]])
    return p


def build_l3b():
    p = Prog("l3b")
    qf = p.inp("qf", (D, T), BF16)
    kfull = p.inp("kfull", (D, 4096), BF16)
    vfull = p.inp("vfull", (4096, D), BF16)
    woT = p.inp("woT", (D, D), F32)
    x3 = p.inp("x3", (D, T), F32)
    gfm = p.inp("g", (D, T), BF16)
    DS = p.inp("DS", (128, 4 * 512), F32)
    KM = p.inp("KM", (128, 4 * 32 * 5), F32)
    o_x4 = p.outp("x4", (D, T), F32)

    ones = p.sb("ones", (128, 128), BF16)
    ld0 = p.sem()
    p.dma("gpsimd", ones[:], p.const(np.ones((128, 128), BF), "cones"), ld0)
    KF = p.sb("KF", (128, NCH * 4096), BF16)
    VF = p.sb("VF", (128, 32 * D), BF16)
    QF = p.sb("QF", (128, NCH * T), BF16)
    WOs = p.sb("WOs", (128, NCH * D), BF16)
    gsl = p.sb("gsl", (128, 512), BF16)
    DSb = p.sb("DSb", (128, 4 * 512), BF16)
    KMb = p.sb("KMb", (128, 640), F32)
    Mt = p.sb("Mt", (128, 512), BF16)
    PT = p.sb("PT", (128, 512), BF16)
    den = p.sb("den", (1, 512), F32)
    denb = p.sb("denb", (1, 512), BF16)
    dbc = p.sb("dbc", (128, 512), BF16)
    AOT = p.sb("AOT", (128, 8 * 512), BF16)
    xsl = p.sb("xsl", (128, 512), F32)
    x4stg = [p.sb(f"x4stg{i}", (128, 512), F32) for i in range(2)]

    p.dma("gpsimd", KF[:].rearrange("p (k n) -> p k n", k=NCH),
          kfull.rearrange("(k p) n -> p k n", p=128), ld0)
    p.dma("gpsimd", VF[:].rearrange("p (k n) -> p k n", k=32),
          vfull.rearrange("(k p) n -> p k n", p=128), ld0)
    p.dma("gpsimd", QF[:].rearrange("p (k n) -> p k n", k=NCH),
          qf.rearrange("(k p) n -> p k n", p=128), ld0)
    p.dma("gpsimd", WOs[:].rearrange("p (k n) -> p k n", k=NCH),
          woT.rearrange("(k p) n -> p k n", p=128), ld0)
    p.dma("gpsimd", DSb[:], DS, ld0)
    p.dma("gpsimd", KMb[:], KM, ld0)
    for e in ENGINES:
        p.q(e, None, wait=((ld0, ld0.n),))

    at = p.sem()

    def astep(eng, fn, extra=()):
        v = at.bump(1)
        p.q(eng, fn, wait=tuple(extra))
        p.q(eng, lambda e: e.drain(), inc=(at, 1))
        for o in ("vector", "scalar", "tensor"):
            if o != eng:
                p.q(o, None, wait=((at, v),))

    stx = [p.sem(), p.sem()]
    xls = p.sem()
    sti = 0
    for qc in range(4):
        for hp in range(8):
            for hh in range(2):
                h = hp * 2 + hh
                hb = hh * 64
                for kt in range(32):
                    ko = qc * 160 + kt * 5
                    astep("vector", lambda e, ko=ko: e.tensor_single_scalar(
                        Mt[:], DSb[:, 0:512], KMb[:, ko + 1:ko + 2], ALU.mult))
                    for d_ in range(1, 4):
                        astep("vector", lambda e, d_=d_, ko=ko: e.scalar_tensor_tensor(
                            Mt[:], DSb[:, d_ * 512:(d_ + 1) * 512], KMb[:, ko + 1 + d_:ko + 2 + d_],
                            Mt[:], ALU.mult, ALU.add))
                    astep("vector", lambda e, ko=ko: e.tensor_single_scalar(
                        Mt[:], Mt[:], KMb[:, ko:ko + 1], ALU.add))
                    astep("tensor", lambda e, hp=hp, hb=hb, qc=qc, kt=kt: e.matmul(
                        p.PSA[:, 0:512],
                        KF[hb:hb + 64, hp * 4096 + kt * 128:hp * 4096 + (kt + 1) * 128],
                        QF[hb:hb + 64, hp * T + qc * 512:hp * T + (qc + 1) * 512],
                        start=True, stop=True))
                    astep("scalar", lambda e: e.activation(out=PT[:], in_=p.PSA[:, 0:512],
                                                           func=AF.Exp, scale=0.125))
                    astep("vector", lambda e: e.tensor_mul(PT[:], PT[:], Mt[:]))
                    astep("tensor", lambda e, kt=kt: e.matmul(
                        p.PSC[0:1, 0:512], ones[:, 0:1], PT[:],
                        start=(kt == 0), stop=(kt == 31)))
                    astep("tensor", lambda e, kt=kt, h=h, hb=hb: e.matmul(
                        p.PSB[hb:hb + 64, 0:512],
                        VF[:, kt * D + h * 64:kt * D + h * 64 + 64],
                        PT[:], start=(kt == 0), stop=(kt == 31)))
                astep("vector", lambda e: e.tensor_scalar(den[:], p.PSC[0:1, 0:512], 1e-30,
                                                          None, ALU.add, ALU.bypass))
                astep("scalar", lambda e: e.activation(out=den[:], in_=den[:], func=AF.Ln))
                astep("scalar", lambda e: e.activation(out=denb[:], in_=den[:], func=AF.Exp,
                                                       scale=-1.0))
                astep("tensor", lambda e: e.matmul(p.PSA[:, 1024:1536], ones[0:1, :],
                                                   denb[:], start=True, stop=True))
                astep("vector", lambda e: e.tensor_copy(dbc[:], p.PSA[:, 1024:1536]))
                astep("vector", lambda e, hp=hp, hb=hb: e.tensor_mul(
                    AOT[hb:hb + 64, hp * 512:(hp + 1) * 512],
                    p.PSB[hb:hb + 64, 0:512], dbc[0:64, :]))
        for do in range(8):
            for hp in range(8):
                astep("tensor", lambda e, do=do, hp=hp: e.matmul(
                    p.PSA[:, 1536:2048],
                    WOs[:, hp * D + do * 128:hp * D + do * 128 + 128],
                    AOT[:, hp * 512:(hp + 1) * 512],
                    start=(hp == 0), stop=(hp == 7)))
            bs = sti % 2
            xv = p.dma("sync", xsl[:], x3[do * 128:(do + 1) * 128, qc * 512:(qc + 1) * 512],
                       xls, wait=((xls, xls.n),) if sti >= 1 else ())
            gv = p.dma("sync", gsl[:], gfm[do * 128:(do + 1) * 128, qc * 512:(qc + 1) * 512], xls)
            astep("vector", lambda e: e.tensor_mul(Mt[:], p.PSA[:, 1536:2048], gsl[:]),
                  extra=((xls, gv),))
            wv = [(xls, xv)]
            if sti >= 2:
                wv.append((stx[bs], 16 * (sti // 2)))
            astep("vector", lambda e, bs=bs: e.tensor_add(x4stg[bs][:], Mt[:].bitcast(F32) if False else Mt[:], xsl[:]),
                  extra=wv)
            p.dma("sync", o_x4[do * 128:(do + 1) * 128, qc * 512:(qc + 1) * 512],
                  x4stg[bs][:], stx[bs], wait=((at, at.n),))
            sti += 1
    finish(p, [stx[0], stx[1]])
    return p


def build_comb():
    p = Prog("comb")
    x2 = p.inp("x2", (D, T), F32)
    s2 = p.inp("s2", (D, T), BF16)
    yin = p.inp("yin", (D, T), F32)
    o_x3 = p.outp("x3", (D, T), F32)
    xa = p.sb("xa", (128, 512), F32)
    ya = p.sb("ya", (128, 512), F32)
    sa = p.sb("sa", (128, 512), BF16)
    ga = p.sb("ga", (128, 512), BF16)
    ost = [p.sb(f"ost{i}", (128, 512), F32) for i in range(2)]
    ld = p.sem()
    ev = p.sem()
    st = [p.sem(), p.sem()]
    i = 0
    for c in range(NCH):
        for (c0, w) in chunks_of(T, 512):
            lv0 = ld.n
            p.dma("sync", xa[:, 0:w], x2[c * 128:(c + 1) * 128, c0:c0 + w], ld,
                  wait=((ev, ev.n),) if i >= 1 else ())
            p.dma("sync", ya[:, 0:w], yin[c * 128:(c + 1) * 128, c0:c0 + w], ld)
            lv = p.dma("sync", sa[:, 0:w], s2[c * 128:(c + 1) * 128, c0:c0 + w], ld)
            gv = ev.bump(1)
            p.q("scalar", lambda e, w=w: e.activation(out=ga[:, 0:w], in_=ya[:, 0:w], func=AF.Silu),
                wait=((ld, lv),))
            p.q("scalar", lambda e: e.drain(), inc=(ev, 1))
            b = i % 2
            wv = [(ev, gv)]
            if i >= 2:
                wv.append((st[b], 16 * (i // 2)))
            p.q("vector", lambda e, w=w: e.tensor_mul(ga[:, 0:w], ga[:, 0:w], sa[:, 0:w]),
                wait=tuple(wv))
            p.q("vector", lambda e, w=w, b=b: e.tensor_add(ost[b][:, 0:w], ga[:, 0:w], xa[:, 0:w]))
            e2v = ev.bump(1)
            p.q("vector", lambda e: e.drain(), inc=(ev, 1))
            p.dma("sync", o_x3[c * 128:(c + 1) * 128, c0:c0 + w], ost[b][:, 0:w], st[b],
                  wait=((ev, e2v),))
            i += 1
    finish(p, [st[0], st[1]])
    return p


_PROGS = {}
_DBG = {}


def _prog(name, builder):
    if name not in _PROGS:
        p = builder()
        _PROGS[name] = (p, p.build())
    return _PROGS[name]


def kernel(**ins):
    z = ins
    x0 = np.asarray(z['L0_attn_representations'])
    C = np.ascontiguousarray

    def fm(a):
        return C(a.T)

    cores = list(range(8))

    def run(name, builder, in_maps):
        p, nc = _prog(name, builder)
        return run_bass_kernel_spmd(nc, in_maps, core_ids=cores).results

    commonA = {"WcT": C(z['L0_attn_Wc'].T),
               "bcT": C(z['L0_attn_bc'][:2048].reshape(16, 128).T),
               "bcv": C(z['L0_attn_bc'][2048:].reshape(1, 8192)),
               "wrT": C(z['L0_ffn_w_router'].T),
               "kfT": C(np.swapaxes(z['L0_ffn_k_ffwd'], 2, 3)),
               "vf": np.asarray(z['L0_ffn_v_ffwd']),
               "wmhT": C(z['L0_ffn_w_mh'].T),
               "wmgT": C(z['L0_ffn_w_mg'].T)}
    ims = []
    for c in cores:
        b, j = c // 2, c % 2
        m = dict(commonA)
        m["x"] = fm(x0[b, j * T:(j + 1) * T, :])
        ims.append(m)
    rA = run("l0a", build_l0a, ims)

    ims = []
    for c in cores:
        b, j = c // 2, c % 2
        kfull = np.concatenate([np.asarray(rA[2 * b]["qk"])[1024:],
                                np.asarray(rA[2 * b + 1]["qk"])[1024:]], axis=1)
        vfull = np.concatenate([np.asarray(rA[2 * b]["v_tok"]),
                                np.asarray(rA[2 * b + 1]["v_tok"])], axis=0)
        ims.append({"q": C(np.asarray(rA[c]["qk"])[:1024]), "kfull": C(kfull),
                    "vfull": C(vfull), "WoT": C(z['L0_attn_Wo'].T),
                    "boT": C(z['L0_attn_bo'].reshape(8, 128).T),
                    "x": fm(x0[b, j * T:(j + 1) * T, :]),
                    "xn": np.asarray(rA[c]["xn"]), "g": np.asarray(rA[c]["g"])})
    rB = run("l0b", build_l0b, ims)
    x1 = [np.asarray(rB[c]["x1"]) for c in cores]
    _DBG["x1"] = x1

    ims = [{"x": x1[c], "Wq": np.asarray(z['L1_attn_Wq']), "Wf": np.asarray(z['L1_attn_Wf']),
            "Wi": np.asarray(z['L1_attn_Wi'])} for c in cores]
    rC1 = run("l1a", build_l1a, ims)
    ims = [{"xn1": np.asarray(rC1[c]["xn1"]),
            "gate_kernel": np.asarray(z['L1_ffn_gate_kernel']),
            "gate_bias": C(np.asarray(z['L1_ffn_gate_bias']).reshape(1, 16)),
            "w_gate": np.asarray(z['L1_ffn_w_gate']), "w_up": np.asarray(z['L1_ffn_w_up']),
            "w_down": np.asarray(z['L1_ffn_w_down']),
            "sw_gate": np.asarray(z['L1_ffn_sw_gate']), "sw_up": np.asarray(z['L1_ffn_sw_up']),
            "sw_down": np.asarray(z['L1_ffn_sw_down'])} for c in cores]
    rC2 = run("l1b", build_l1b, ims)
    ims = []
    for c in cores:
        b, j = c // 2, c % 2
        Sin = np.asarray(rC1[2 * b]["Sout"]) if j == 1 else np.zeros((D, 128), np.float32)
        ims.append({"o_loc": np.asarray(rC1[c]["o_loc"]), "qdp": np.asarray(rC1[c]["qdp"]),
                    "Sin": C(Sin),
                    "gwT": C(np.asarray(z['L1_attn_g_weight']).reshape(8, 128).T),
                    "Wo": np.asarray(z['L1_attn_Wo']), "gffn": np.asarray(rC2[c]["gffn"]),
                    "x1": x1[c]})
    rC3 = run("l1c", build_l1c, ims)
    x2 = [np.asarray(rC3[c]["x2"]) for c in cores]
    _DBG["x2"] = x2
    x2g = np.stack([np.concatenate([x2[2 * b].T, x2[2 * b + 1].T], 0) for b in range(4)])

    ims = []
    for c in cores:
        b, j = c // 2, c % 2
        g = np.arange(j * T - 32, (j + 1) * T + 32) % 4096
        ims.append({"x2f": C(x2g[b][g, :].T), "gWT": C(z['L2_attn_gate_W'].T),
                    "vWT": C(z['L2_attn_value_W'].T),
                    "gamT": C(z['L2_attn_gamma'].reshape(8, 128).T),
                    "betT": C(z['L2_attn_beta'].reshape(8, 128).T)})
    rD = run("l2a", build_l2a, ims)
    s2 = [np.asarray(rD[c]["s2"]) for c in cores]
    xin_glob = np.stack([np.concatenate([np.asarray(rD[2 * b]["xn2"]).T.astype(np.float32),
                                         np.asarray(rD[2 * b + 1]["xn2"]).T.astype(np.float32)], 0)
                         for b in range(4)])

    for li in range(6):
        ims = []
        for c in cores:
            b, j = c // 2, c % 2
            ims.append(l2f_inputs(xin_glob, b, j, z, li))
        rE = run("l2f", build_l2f, ims)
        for c in cores:
            b, j = c // 2, c % 2
            xin_glob[b, j * T:(j + 1) * T, :] = np.asarray(rE[c]["xout"]).T
    ims = []
    for c in cores:
        b, j = c // 2, c % 2
        ims.append({"x2": x2[c], "s2": s2[c], "yin": fm(xin_glob[b, j * T:(j + 1) * T, :])})
    rX3 = run("comb", build_comb, ims)
    x3 = [np.asarray(rX3[c]["x3"]) for c in cores]
    _DBG["x3"] = x3
    _DBG["s2"] = s2
    _DBG["l2out"] = [fm(xin_glob[c // 2, (c % 2) * T:(c % 2 + 1) * T, :]) for c in cores]

    DH = 64
    kk2 = np.arange(32)
    invf = 10000.0 ** (kk2 * (-2.0 / DH))
    ims = []
    for c in cores:
        b, j = c // 2, c % 2
        pos = np.arange(j * T, (j + 1) * T)
        th = pos[None, :] * invf[:, None]
        cos_t = np.cos(th); sin_t = np.sin(th)
        RC64 = np.repeat(cos_t, 2, axis=0)
        RS64 = np.repeat(sin_t, 2, axis=0)
        ims.append({"x": x3[c], "wqT": C(z['L3_attn_wq'].T), "wkT": C(z['L3_attn_wk'].T),
                    "wvT": C(z['L3_attn_wv'].T),
                    "RC": C(np.concatenate([RC64, RC64], 0).astype(np.float32)),
                    "RS": C(np.concatenate([RS64, RS64], 0).astype(np.float32)),
                    "W1": np.asarray(z['L3_ffn_W1']),
                    "b1T": C(np.asarray(z['L3_ffn_b1']).reshape(8, 32, 128).transpose(0, 2, 1)),
                    "W2": np.asarray(z['L3_ffn_W2']),
                    "b2r": np.asarray(z['L3_ffn_b2']),
                    "b2flat": C(np.asarray(z['L3_ffn_b2']).reshape(1, 8192)),
                    "Wr": np.asarray(z['L3_ffn_Wr']),
                    "brr": C(np.asarray(z['L3_ffn_br']).reshape(1, 8))})
    rF = run("l3a", build_l3a, ims)

    o_ = np.arange(128)
    DS = np.zeros((128, 4 * 512), np.float32)
    for d_ in range(4):
        qq = np.arange(512)[None, :]
        DS[:, d_ * 512:(d_ + 1) * 512] = (qq >= (d_ * 128 + o_)[:, None]).astype(np.float32)
    ims = []
    for c in cores:
        b, j = c // 2, c % 2
        kfull = np.concatenate([np.asarray(rF[2 * b]["kf"]), np.asarray(rF[2 * b + 1]["kf"])], 1)
        vfull = np.concatenate([np.asarray(rF[2 * b]["vt"]), np.asarray(rF[2 * b + 1]["vt"])], 0)
        KM = np.zeros((128, 640), np.float32)
        for qc in range(4):
            g_ = j * 4 + qc
            for kt in range(32):
                ko = qc * 160 + kt * 5
                if kt < 4 * g_:
                    KM[:, ko] = 1.0
                elif kt < 4 * (g_ + 1):
                    KM[:, ko + 1 + (kt - 4 * g_)] = 1.0
        ims.append({"qf": np.asarray(rF[c]["qf"]), "kfull": C(kfull), "vfull": C(vfull),
                    "woT": C(z['L3_attn_wo'].T), "x3": x3[c], "g": np.asarray(rF[c]["gffn"]),
                    "DS": DS, "KM": KM})
    rG = run("l3b", build_l3b, ims)

    _DBG["qf"] = [np.asarray(rF[c]["qf"]) for c in cores]
    _DBG["gffn3"] = [np.asarray(rF[c]["gffn"]) for c in cores]
    out = np.empty((4, 4096, 1024), np.float32)
    for c in cores:
        b, j = c // 2, c % 2
        out[b, j * T:(j + 1) * T, :] = np.asarray(rG[c]["x4"]).T
    return out
